# version 03 (final)
# baseline (speedup 1.0000x reference)
import numpy as np
import ml_dtypes
import concourse.bass as bass
import concourse.mybir as mybir
from concourse.bass_utils import run_bass_kernel_spmd
from contextlib import ExitStack

F32 = mybir.dt.float32
BF16 = mybir.dt.bfloat16
AF = mybir.ActivationFunctionType
ALU = mybir.AluOpType
AX = mybir.AxisListType
BF = ml_dtypes.bfloat16

NCORES = 8
B = 2
N = 2048
D = 1024
T = B * N
NT = T // 128
NTB = N // 128
EPS_LN = 1e-5
NSLAB = 39

CONST_LIST = [EPS_LN if False else 1e-5, -0.5, 0.0, 1.0, -1.0, 0.125, 8.0, 2.0, 0.5, -2.0, 1e-8, -0.125, 0.03125]

ENGINES = ("sync", "gpsimd", "vector", "scalar", "tensor")


class Sem:
    __slots__ = ("name", "cnt", "h")

    def __init__(self, name):
        self.name = name
        self.cnt = 0
        self.h = None


class Prog:
    def __init__(self):
        self.stream = {e: [] for e in ENGINES}
        self.sems = {}

    def sem(self, name):
        if name not in self.sems:
            self.sems[name] = Sem(name)
        return self.sems[name]

    def op(self, eng, fn, waits=(), inc=None, n=1):
        tgt = None
        if inc is not None:
            inc.cnt += n
            tgt = inc.cnt
        wl = tuple(waits)

        def item(e, _fn=fn, _w=wl, _inc=inc, _n=n):
            for s, v in _w:
                e.wait_ge(s.h, v)
            r = _fn(e)
            if _inc is not None:
                r.then_inc(_inc.h, _n)
        self.stream[eng].append(item)
        return tgt

    def wait(self, eng, waits):
        def item(e, _w=tuple(waits)):
            for s, v in _w:
                e.wait_ge(s.h, v)
        self.stream[eng].append(item)

    def replay(self, nc, es):
        for s in self.sems.values():
            s.h = es.enter_context(nc.semaphore(s.name))
        block = es.enter_context(nc.Block())
        for eng in ENGINES:
            ops = self.stream[eng]
            section = getattr(block, eng)

            def body(e, _ops=ops):
                for it in _ops:
                    it(e)
            section(body)


def ap_mk(t, col0, dims, part0=0, nparts=128):
    pitch = t.shape[1]
    return bass.AP(t, part0 * pitch + col0, [[pitch, nparts]] + [list(d) for d in dims])


class Env:
    def __init__(self, nc, P, es, dram):
        self.nc = nc
        self.P = P
        self.es = es
        self.dram = dram
        self.qX = [P.sem(f"qX{i}") for i in range(4)]
        self.qO = [P.sem(f"qO{i}") for i in range(4)]
        self.qM = [P.sem(f"qM{i}") for i in range(2)]
        self.qF = [P.sem(f"qF{i}") for i in range(3)]
        self.qC = P.sem("qC")
        self.qGL = [P.sem(f"qG{i}") for i in range(4)]
        self._rrM = 0
        self._rrG = 0
        self.eT = P.sem("eT")
        self.eV = P.sem("eV")
        self.eS = P.sem("eS")
        self.eG = P.sem("eG")
        self.AB = es.enter_context(nc.sbuf_tensor("arena_bf", [128, NSLAB * 2048], BF16))
        self.AF = es.enter_context(nc.sbuf_tensor("arena_f32", [128, 8768], F32))
        self.ident = es.enter_context(nc.sbuf_tensor("ident_bf", [128, 128], BF16))
        self.rows = es.enter_context(nc.sbuf_tensor("rows_bf", [1, 6144], BF16))
        self.consts = es.enter_context(nc.sbuf_tensor("consts_f32", [128, 64], F32))
        self.cones = es.enter_context(nc.sbuf_tensor("cones_bf", [128, 8], BF16))
        self.selT = es.enter_context(nc.sbuf_tensor("selT_bf", [16, 2048], BF16))
        self.PSB = es.enter_context(nc.psum_tensor("psb", [128, 2048], F32))
        self.PS = [self.PSB[:, 512 * i:512 * (i + 1)] for i in range(4)]
        self.PS += [es.enter_context(nc.psum_tensor(f"ps{i}", [128, 512], F32)) for i in (4, 5)]
        self.PT = [es.enter_context(nc.psum_tensor(f"pt{i}", [128, 1024], BF16)) for i in range(2)]
        self._ps_guard = [None] * 6
        self._pt_guard = [None] * 2
        self._xt_guard = [None] * 4
        self._ot_guard = [None] * 4
        self._af_guard = {}
        for fn in [
            lambda e: e.dma_start(out=self.ident[:], in_=dram["identity128"]),
            lambda e: e.dma_start(out=self.rows[0:1, 0:512], in_=dram["ones512"]),
            lambda e: e.dma_start(out=self.cones[:], in_=dram["onescol"]),
            lambda e: e.dma_start(out=self.consts[:, 0:len(CONST_LIST)],
                                  in_=dram["constf32"]),
        ]:
            P.op("sync", fn, inc=self.qC, n=16)
        for i, v in enumerate(CONST_LIST):
            nc.const_aps.register(F32, float(v), self.consts[:, i:i + 1])
        self.c_ident = [(self.qC, self.qC.cnt)]
        for _eng in ENGINES:
            P.wait(_eng, self.c_ident)

    def dma(self, queue, fn, waits=(), slot=None):
        if queue == "sync":
            if slot is not None:
                kind, i = slot
                if kind == "f":
                    lane = self.qF[i % 3]
                else:
                    lane = (self.qX if kind == "x" else self.qO)[i % 4]
            else:
                lane = self.qM[self._rrM % 2]
                self._rrM += 1
        else:
            lane = self.qGL[self._rrG % 4]
            self._rrG += 1
        tgt = self.P.op(queue, fn, waits=waits, inc=lane, n=16)
        return (lane, tgt)

    def qs_now(self):
        return [(s, s.cnt) for s in self.qM if s.cnt > 0]

    def qg_now(self):
        return [(s, s.cnt) for s in self.qGL if s.cnt > 0]

    def all_dma_waits(self):
        return [(s, s.cnt) for s in self.qX + self.qO + self.qM + self.qF + self.qGL + [self.qC] if s.cnt > 0]

    def slab(self, i, j0=0, n=2048):
        return self.AB[:, i * 2048 + j0: i * 2048 + j0 + n]

    def sview(self, i, j0, dims, part0=0, nparts=128):
        return ap_mk(self.AB, i * 2048 + j0, dims, part0, nparts)

    def xt(self, j):
        return self.AF[:, (j % 4) * 1024: (j % 4) * 1024 + 1024]

    def ot(self, j):
        return self.AF[:, 4096 + (j % 4) * 1024: 4096 + (j % 4) * 1024 + 1024]

    def stat(self, k, n=1):
        return self.AF[:, 8192 + k: 8192 + k + n]

    def bias_col(self, i, n=1):
        return self.AF[:, 8256 + i: 8256 + i + n]

    def ones1(self, m=128):
        return self.rows[0:1, 0:m]

    def row(self, idx, col0, n):
        assert col0 + n <= 1280
        return self.rows[0:1, 512 + idx * 1280 + col0: 512 + idx * 1280 + col0 + n]

    def ps_wait(self, bank):
        g = self._ps_guard[bank]
        return [g] if g is not None else []

    def ps_done(self, bank, sem, tgt):
        self._ps_guard[bank] = (sem, tgt)

    def pt_wait(self, bank):
        g = self._pt_guard[bank]
        return [g] if g is not None else []

    def pt_done(self, bank, sem, tgt):
        self._pt_guard[bank] = (sem, tgt)

    def xt_wait(self, j):
        g = self._xt_guard[j % 4]
        return [g] if g is not None else []

    def xt_done(self, j, sem, tgt):
        self._xt_guard[j % 4] = (sem, tgt)

    def ot_wait(self, j):
        g = self._ot_guard[j % 4]
        return [g] if g is not None else []

    def ot_done(self, j, sem, tgt):
        self._ot_guard[j % 4] = (sem, tgt)

    def debug_dump(self, name, ap, waits, shape=None, dtype=None):
        if not getattr(self, "debug", False):
            return
        shape = shape or list(ap.shape)
        dt = dtype or (BF16 if "bf" in repr(ap) else None)
        dt = ap.tensor.dtype if hasattr(ap, "tensor") else F32
        d = self.nc.dram_tensor("dbg_" + name, list(shape), dt, kind="ExternalOutput").ap()
        self.dma("sync", lambda e, _d=d, _a=ap: e.dma_start(out=_d, in_=_a), waits=waits)



def emit_ln_tile(env, src_dram, j, dst_bf_ap, mode=None, gamma_col=None):
    P, eV, eS = env.P, env.eV, env.eS
    xt = env.xt(j)
    ld = env.dma("sync", lambda e, _x=xt, _s=src_dram, _j=j: e.dma_start(
        out=_x, in_=_s[_j * 128:(_j + 1) * 128, :]), waits=env.xt_wait(j), slot=("x", j))
    bnr = env.stat(0, 12)
    bno = env.stat(12, 2)
    P.op("vector", lambda e, _x=xt[:, 0:512], _o=bnr[:, 0:6]: e.bn_stats(_o, _x),
         waits=[ld], inc=eV)
    t1 = P.op("vector", lambda e, _x=xt[:, 512:1024], _o=bnr[:, 6:12]: e.bn_stats(_o, _x),
              inc=eV)
    t2 = P.op("vector", lambda e, _i=bnr, _o=bno: e.bn_aggr(_o, _i),
              waits=[(eV, t1)], inc=eV)
    lnv = env.stat(14, 1)
    rstd = env.stat(15, 1)
    eps = 0.0 if mode == 'l0' else EPS_LN
    s1 = P.op("scalar", lambda e, _i=bno, _o=lnv, _b=eps: e.activation(
        _o, _i[:, 1:2], AF.Ln, bias=_b), waits=[(eV, t2)] + env.c_ident, inc=eS)
    s2 = P.op("scalar", lambda e, _i=lnv, _o=rstd: e.activation(
        _o, _i, AF.Exp, scale=-0.5), waits=[(eS, s1)], inc=eS)
    xn = P.op("vector", lambda e, _x=xt, _o=dst_bf_ap, _m=bno, _r=rstd: e.tensor_scalar(
        _o, _x, _m[:, 0:1], _r[:, 0:1], ALU.subtract, ALU.mult),
        waits=[(eS, s2)], inc=eV)
    env.xt_done(j, eV, xn)
    return ld, xn


def emit_transpose_8(env, src_tile_ap, dst_group_ap, wait, pt_idx=0, nblk=8,
                     evac_eng="vector"):
    P, eT = env.P, env.eT
    pt = env.PT[pt_idx]
    for k in range(nblk):
        w = list(wait) + env.pt_wait(pt_idx) + env.c_ident if k == 0 else []
        P.op("tensor", lambda e, _i=src_tile_ap[:, k * 128:(k + 1) * 128],
             _o=pt[:, k * 128:(k + 1) * 128]:
             e.transpose(_o, _i, env.ident[:]), waits=w,
             inc=eT if k == nblk - 1 else None)
    tT = eT.cnt
    if evac_eng == "vector":
        ev = P.op("vector", lambda e, _i=pt[:, 0:nblk * 128], _o=dst_group_ap:
                  e.tensor_copy(_o, _i), waits=[(eT, tT)], inc=env.eV)
        env.pt_done(pt_idx, env.eV, ev)
    else:
        ev = P.op("scalar", lambda e, _i=pt[:, 0:nblk * 128], _o=dst_group_ap:
                  e.copy(_o, _i), waits=[(eT, tT)], inc=env.eS)
        env.pt_done(pt_idx, env.eS, ev)
    return ev


def emit_ln_fm(env, src_dram, tiles, dst_slab, scratch_slab, mode=None):
    W = len(tiles) * 128
    assert W * 8 <= 2048 * 8
    ev = None
    for idx, j in enumerate(tiles):
        tm = env.slab(scratch_slab, (idx % 2) * 1024, 1024)
        ld, xn = emit_ln_tile(env, src_dram, j, tm, mode)
        dst = env.sview(dst_slab, idx * 128, [[W, 8], [1, 128]])
        ev = emit_transpose_8(env, tm, dst, [(env.eV, xn)], pt_idx=idx % 2)
    return ev


def emit_ln_fm_batch(env, src_dram, b, slab0, scratch_slab, mode=None):
    ev = None
    for jj in range(NTB):
        j = b * NTB + jj
        tm = env.slab(scratch_slab, (jj % 2) * 1024, 1024)
        ld, xn = emit_ln_tile(env, src_dram, j, tm, mode)
        dst = ap_mk(env.AB, slab0 * 2048 + jj * 128, [[2048, 8], [1, 128]])
        ev = emit_transpose_8(env, tm, dst, [(env.eV, xn)], pt_idx=jj % 2)
    return ev


def emit_residual_store(env, src_dram, dst_dram, j, psums, waits):
    P, eV = env.P, env.eV
    xt = env.xt(j)
    ld = env.dma("sync", lambda e, _x=xt, _s=src_dram, _j=j: e.dma_start(
        out=_x, in_=_s[_j * 128:(_j + 1) * 128, :]), waits=env.xt_wait(j), slot=("x", j))
    tv = None
    for half, (ps, w) in enumerate(zip(psums, waits)):
        ww = [ld] + list(w)
        if half == 0:
            ww += env.ot_wait(j)
        tv = P.op("vector", lambda e, _o=env.ot(j)[:, half * 512:(half + 1) * 512],
                  _p=ps, _x=xt[:, half * 512:(half + 1) * 512]:
                  e.scalar_tensor_tensor(_o, _p, 1.0, _x, ALU.mult, ALU.add),
                  waits=ww, inc=eV)
    env.xt_done(j, eV, tv)
    st = env.dma("sync", lambda e, _o=env.ot(j), _d=dst_dram, _j=j: e.dma_start(
        out=_d[_j * 128:(_j + 1) * 128, :], in_=_o), waits=[(eV, tv)], slot=("o", j))
    env.ot_done(j, st[0], st[1])
    return st, tv


def load_weight_bf16(env, name, placements):
    dram_ap = env.dram[name]
    R, C = dram_ap.shape
    nk = (R + 127) // 128
    assert len(placements) >= nk, (name, R, C, len(placements))
    env.P.wait("gpsimd", env.qg_now())
    last = {}
    for k in range(nk):
        r0, r1 = k * 128, min((k + 1) * 128, R)
        sl, col = placements[k]
        dst = ap_mk(env.AB, sl * 2048 + col, [[1, C]], nparts=r1 - r0)
        s, t = env.dma("gpsimd", lambda e, _d=dst, _s=dram_ap[r0:r1, :]:
                       e.dma_start(out=_d, in_=_s))
        last[s.name] = (s, t)
    return env.qg_now()


def load_bias_cols(env, name, af_col, ncols):
    dst = env.AF[:, af_col:af_col + ncols]
    env.P.wait("sync", env.qs_now())
    env.dma("sync", lambda e, _d=dst, _s=env.dram[name]: e.dma_start(out=_d, in_=_s))
    return env.qs_now()


def load_row(env, name, row_idx, col0=0):
    src = env.dram[name]
    n = src.shape[-1]
    dst = env.row(row_idx, col0, n)
    env.P.wait("gpsimd", env.qg_now())
    env.dma("gpsimd", lambda e, _d=dst, _s=src: e.dma_start(out=_d, in_=_s))
    return env.qg_now()



def emit_l2ffn(env, src, dst, wp):
    P = env.P
    w1t = load_weight_bf16(env, wp + "W1T", [(2 * k, 0) for k in range(8)])
    w2t = load_weight_bf16(env, wp + "W2T", [(16 + ic // 2, (ic % 2) * 1024) for ic in range(32)])
    b1 = load_bias_cols(env, wp + "b1col", 8256, 32)
    b2 = load_row(env, wp + "b2row", 0)
    for b in range(B):
        for g in range(8):
            tiles = [b * NTB + 2 * g, b * NTB + 2 * g + 1]
            xnsl = 32 + (g % 2)
            lnv = emit_ln_fm(env, src, tiles, xnsl, 38)
            h1ev = []
            for ic in range(32):
                bank = ic % 4
                ps = env.PS[bank]
                for kc in range(8):
                    w = []
                    if kc == 0:
                        w = [(env.eV, lnv)] + w1t + env.ps_wait(bank)
                    lhsT = env.AB[:, 2 * kc * 2048 + ic * 128: 2 * kc * 2048 + (ic + 1) * 128]
                    rhs = env.slab(xnsl, kc * 256, 256)
                    P.op("tensor", lambda e, _o=ps[:, 0:256], _l=lhsT, _r=rhs,
                         _s=(kc == 0), _p=(kc == 7):
                         e.matmul(_o, _l, _r, start=_s, stop=_p),
                         waits=w, inc=env.eT if kc == 7 else None)
                tmm = env.eT.cnt
                h1sl = 34 + ic // 8
                h1col = (ic % 8) * 256
                ev = P.op("vector", lambda e, _o=env.slab(h1sl, h1col, 256),
                          _i=ps[:, 0:256], _b=env.AF[:, 8256 + ic:8256 + ic + 1]:
                          e.tensor_scalar(_o, _i, _b, None, ALU.add),
                          waits=[(env.eT, tmm)] + b1, inc=env.eV)
                env.ps_done(bank, env.eV, ev)
                h1ev.append(ev)
            h1all = env.eV.cnt
            chain_tgt = {}
            for ci, (ts, nch) in enumerate([(t_, n_) for t_ in range(2) for n_ in range(2)]):
                bank = ci % 4
                ps = env.PS[bank]
                for ic in range(32):
                    w = []
                    if ic == 0:
                        w = [(env.eV, h1all)] + w2t + env.ps_wait(bank)
                    lhsT = env.slab(34 + ic // 8, (ic % 8) * 256 + ts * 128, 128)
                    rhs = env.slab(16 + ic // 2, (ic % 2) * 1024 + nch * 512, 512)
                    P.op("tensor", lambda e, _o=ps[:], _l=lhsT, _r=rhs, _s=(ic == 0):
                         e.matmul(_o, _l, _r, start=_s, stop=False), waits=w)
                P.op("tensor", lambda e, _o=ps[:], _l=env.ones1(128),
                     _r=env.row(0, nch * 512, 512):
                     e.matmul(_o, _l, _r, start=False, stop=True),
                     waits=b2, inc=env.eT)
                chain_tgt[(ts, nch)] = env.eT.cnt
            for ts in range(2):
                j = tiles[ts]
                pA, pB = env.PS[ts * 2][:], env.PS[ts * 2 + 1][:]
                st, tv = emit_residual_store(
                    env, src, dst, j,
                    [pA, pB],
                    [[(env.eT, chain_tgt[(ts, 0)])], [(env.eT, chain_tgt[(ts, 1)])]])
                env.ps_done(ts * 2, env.eV, tv)
                env.ps_done(ts * 2 + 1, env.eV, tv)



def emit_barrier(env):
    P = env.P
    tgts = env.all_dma_waits() + [(s, s.cnt) for s in (env.eT, env.eV, env.eS)]
    for eng in ENGINES:
        P.wait(eng, tgts)
    env._ps_guard = [None] * 6
    env._pt_guard = [None] * 2
    env._xt_guard = [None] * 4
    env._ot_guard = [None] * 4



def emit_l3b(env, src, dst, wp):
    P = env.P
    for b in range(B):
        env.P.wait("gpsimd", [(env.eT, env.eT.cnt)])
        wq = load_weight_bf16(env, wp + "WqT", [(24 + k, 0) for k in range(8)])
        lnv = None
        for jj in range(NTB):
            j = b * NTB + jj
            tm = env.slab(8 + jj // 2, (jj % 2) * 1024, 1024)
            ld, xn = emit_ln_tile(env, src, j, tm)
            dst_fm = ap_mk(env.AB, 0 * 2048 + jj * 128, [[2048, 8], [1, 128]])
            lnv = emit_transpose_8(env, tm, dst_fm, [(env.eV, xn)], pt_idx=jj % 2)
        for oc in range(8):
            for nc_ in range(4):
                bank = (oc * 4 + nc_) % 4
                ps = env.PS[bank]
                for kc in range(8):
                    w = []
                    if kc == 0:
                        w = [(env.eV, lnv)] + wq + env.ps_wait(bank)
                    lhsT = env.AB[:, (24 + kc) * 2048 + oc * 128:(24 + kc) * 2048 + oc * 128 + 128]
                    rhs = env.slab(kc, nc_ * 512, 512)
                    P.op("tensor", lambda e, _o=ps[:], _l=lhsT, _r=rhs, _s=(kc == 0),
                         _p=(kc == 7): e.matmul(_o, _l, _r, start=_s, stop=_p),
                         waits=w, inc=env.eT if kc == 7 else None)
                tmm = env.eT.cnt
                ev = P.op("vector", lambda e, _o=env.slab(16 + oc, nc_ * 512, 512),
                          _i=ps[:]: e.tensor_copy(_o, _i),
                          waits=[(env.eT, tmm)], inc=env.eV)
                env.ps_done(bank, env.eV, ev)
        qv = env.eV.cnt
        if b == 0:
            env.debug_dump("xnfm0", env.slab(0, 0, 2048), [(env.eV, lnv)])
            env.debug_dump("xntm0", env.slab(8, 0, 1024), [(env.eV, lnv)])
            env.debug_dump("qfm0", env.slab(16, 0, 2048), [(env.eV, qv)])
        for sh in range(2):
            for tc in range(16):
                for sc in range(2):
                    bank = (tc * 2 + sc) % 4
                    ps = env.PS[bank]
                    for oc in range(8):
                        w = []
                        if oc == 0:
                            w = [(env.eV, qv)] + env.ps_wait(bank)
                        lhsT = env.slab(oc, tc * 128, 128)
                        rhs = env.slab(16 + oc, sh * 1024 + sc * 512, 512)
                        P.op("tensor", lambda e, _o=ps[:], _l=lhsT, _r=rhs,
                             _s=(oc == 0), _p=(oc == 7):
                             e.matmul(_o, _l, _r, start=_s, stop=_p),
                             waits=w, inc=env.eT if oc == 7 else None)
                    tmm = env.eT.cnt
                    ev = P.op("vector", lambda e,
                              _o=env.slab(24 + tc // 2, (tc % 2) * 1024 + sc * 512, 512),
                              _i=ps[:]: e.tensor_copy(_o, _i),
                              waits=[(env.eT, tmm)], inc=env.eV)
                    env.ps_done(bank, env.eV, ev)
            pv = env.eV.cnt
            if b == 0 and sh == 0:
                env.debug_dump("preT0", env.slab(24, 0, 2048), [(env.eV, pv)])
            for st_ in range(8):
                j = b * NTB + sh * 8 + st_
                tgts = []
                for nch in range(2):
                    bank = st_ % 2 * 2 + nch
                    ps = env.PS[bank]
                    for tc in range(16):
                        w = []
                        if tc == 0:
                            w = [(env.eV, pv)] + env.ps_wait(bank)
                        lhsT = env.AB[:, (24 + tc // 2) * 2048 + (tc % 2) * 1024 + st_ * 128:
                                      (24 + tc // 2) * 2048 + (tc % 2) * 1024 + st_ * 128 + 128]
                        rhs = env.slab(8 + (b * 0) + tc // 2, (tc % 2) * 1024 + nch * 512, 512)
                        P.op("tensor", lambda e, _o=ps[:], _l=lhsT, _r=rhs,
                             _s=(tc == 0), _p=(tc == 15):
                             e.matmul(_o, _l, _r, start=_s, stop=_p),
                             waits=w, inc=env.eT if tc == 15 else None)
                    tgts.append(env.eT.cnt)
                st2, tv = emit_residual_store(
                    env, src, dst, j,
                    [env.PS[st_ % 2 * 2][:], env.PS[st_ % 2 * 2 + 1][:]],
                    [[(env.eT, tgts[0])], [(env.eT, tgts[1])]])
                env.ps_done(st_ % 2 * 2, env.eV, tv)
                env.ps_done(st_ % 2 * 2 + 1, env.eV, tv)


def prep_l3b(inputs, core, dram):
    dram["l3b_WqT"] = np.ascontiguousarray(inputs["L3_attn_b_W_q"].T)




def emit_l0f(env, src, dst, wp):
    P = env.P
    load_weight_bf16(env, wp + "bigG1R", [(8, 0), (8, 128), (8, 256), (8, 384)])
    load_weight_bf16(env, wp + "bigG2R", [(8, 512), (8, 576), (8, 640), (8, 704)])
    load_weight_bf16(env, wp + "bigG1I", [(8, 768), (8, 896), (8, 1024), (8, 1152)])
    g2i = load_weight_bf16(env, wp + "bigG2I", [(8, 1280), (8, 1344), (8, 1408), (8, 1472)])
    brow = load_row(env, wp + "biasrow", 0)
    acc = env.slab(9, 0, 512)
    tmp = env.slab(9, 512, 512)
    t = None
    for qi, (g1c0, g2c0) in enumerate([(0, 512), (768, 1280)]):
        for rr in range(4):
            for c in range(4):
                a = ap_mk(env.AB, 8 * 2048 + g1c0 + rr * 128 + 32 * c, [[1, 32], [0, 16]])
                bb = ap_mk(env.AB, 8 * 2048 + g2c0 + rr * 64 + 16 * c, [[0, 32], [1, 16]])
                if c == 0:
                    w = list(g2i) if t is None else list(g2i) + [(env.eV, t)]
                    t = P.op("vector", lambda e, _o=acc, _a=a, _b=bb:
                             e.tensor_tensor(_o, _a, _b, ALU.mult), waits=w, inc=env.eV)
                else:
                    t = P.op("vector", lambda e, _o=tmp, _a=a, _b=bb:
                             e.tensor_tensor(_o, _a, _b, ALU.mult),
                             waits=[(env.eV, t)], inc=env.eV)
                    t = P.op("vector", lambda e, _o=acc, _a=acc, _b=tmp:
                             e.tensor_tensor(_o, _a, _b, ALU.add),
                             waits=[(env.eV, t)], inc=env.eV)
            if qi == 0:
                spots = [((rr, 0), False), ((rr + 4, 512), False)]
            else:
                spots = [((rr, 512), False), ((rr + 4, 0), True)]
            for (r, c0), neg in spots:
                sl, col = 10 + r // 2, (r % 2) * 1024 + c0
                if neg:
                    t = P.op("vector", lambda e, _o=env.slab(sl, col, 512), _i=acc:
                             e.tensor_scalar(_o, _i, -1.0, None, ALU.mult),
                             waits=[(env.eV, t)], inc=env.eV)
                else:
                    t = P.op("vector", lambda e, _o=env.slab(sl, col, 512), _i=acc:
                             e.tensor_copy(_o, _i), waits=[(env.eV, t)], inc=env.eV)
    mev = t
    for b in range(B):
        lnv = emit_ln_fm_batch(env, src, b, 0, 14)
        for jj in range(NTB):
            j = b * NTB + jj
            tgts = []
            for nch in range(2):
                bank = (jj % 2) * 2 + nch
                ps = env.PS[bank]
                for kc in range(8):
                    w = []
                    if kc == 0:
                        w = [(env.eV, lnv), (env.eV, mev)] + brow + env.ps_wait(bank)
                    lhsT = env.slab(kc, jj * 128, 128)
                    rhs = env.slab(10 + kc // 2, (kc % 2) * 1024 + nch * 512, 512)
                    P.op("tensor", lambda e, _o=ps[:], _l=lhsT, _r=rhs, _s=(kc == 0):
                         e.matmul(_o, _l, _r, start=_s, stop=False), waits=w)
                P.op("tensor", lambda e, _o=ps[:], _l=env.ones1(128),
                     _r=env.row(0, nch * 512, 512):
                     e.matmul(_o, _l, _r, start=False, stop=True), inc=env.eT)
                tgts.append(env.eT.cnt)
            st, tv = emit_residual_store(
                env, src, dst, j,
                [env.PS[(jj % 2) * 2][:], env.PS[(jj % 2) * 2 + 1][:]],
                [[(env.eT, tgts[0])], [(env.eT, tgts[1])]])
            env.ps_done((jj % 2) * 2, env.eV, tv)
            env.ps_done((jj % 2) * 2 + 1, env.eV, tv)


def prep_l0f_make(pfx_in, pfx_out):
    def prep(inputs, core, dram):
        G1R = np.asarray(inputs[pfx_in + "G1_R"], np.float32)[0]
        G2R = np.asarray(inputs[pfx_in + "G2_R"], np.float32)[:, :, 0, :]
        G1I = np.asarray(inputs[pfx_in + "G1_I"], np.float32)[0]
        G2I = np.asarray(inputs[pfx_in + "G2_I"], np.float32)[:, :, 0, :]
        big1 = lambda G: np.ascontiguousarray(
            np.repeat(G.transpose(2, 1, 0).reshape(32, 128), 16, axis=0))
        big2 = lambda G: np.ascontiguousarray(
            np.tile(G.transpose(2, 0, 1).reshape(16, 64), (32, 1)))
        dram[pfx_out + "bigG1R"] = big1(G1R)
        dram[pfx_out + "bigG2R"] = big2(G2R)
        dram[pfx_out + "bigG1I"] = big1(G1I)
        dram[pfx_out + "bigG2I"] = big2(G2I)
        dram[pfx_out + "biasrow"] = np.concatenate(
            [np.asarray(inputs[pfx_in + "bias_real"], np.float32),
             np.asarray(inputs[pfx_in + "bias_imag"], np.float32)]).reshape(1, 1024)
    return prep



L0A_SKIP = set()


def emit_l0a(env, src, dst, wp):
    P = env.P
    SC = 8448
    env._l0a_scratch_guard = {}
    gcol = load_bias_cols(env, wp + "gammacol", 8256, 8)
    env.dma("sync", lambda e: e.dma_start(out=env.AF[0:1, 8264:8280],
            in_=env.dram[wp + "temp"]))
    tcol = env.qs_now()
    at = P.op("scalar", lambda e: e.activation(env.AF[0:1, 8280:8296],
              env.AF[0:1, 8264:8280], AF.Exp), waits=tcol + env.c_ident,
              inc=env.eS)
    at2 = P.op("vector", lambda e: e.tensor_copy(env.row(1, 0, 16), env.AF[0:1, 8280:8296]),
               waits=[(env.eS, at)], inc=env.eV)
    at3 = P.op("tensor", lambda e: e.matmul(env.PS[5][0:64, 0:16], env.ones1(64),
               env.row(1, 0, 16), start=True, stop=True),
               waits=[(env.eV, at2)] + env.ps_wait(5), inc=env.eT)
    at4 = P.op("vector", lambda e: e.tensor_scalar(env.AF[0:64, 8296:8312],
               env.PS[5][0:64, 0:16], 8.0, None, ALU.mult), waits=[(env.eT, at3)], inc=env.eV)
    env.ps_done(5, env.eV, at4)
    acol = lambda h: env.AF[0:64, 8296 + h:8296 + h + 1]

    for b in range(B):
        P.wait("gpsimd", [(env.eT, env.eT.cnt)])
        wq = load_weight_bf16(env, wp + "wqkv", WQKV_PLACE)
        gev = None
        for kc in range(8):
            sl, col = WQKV_PLACE[kc]
            gev = P.op("vector", lambda e, _a=ap_mk(env.AB, sl * 2048 + col, [[1, 3072]]),
                       _g=env.AF[:, 8256 + kc:8256 + kc + 1]:
                       e.tensor_scalar(_a, _a, _g, None, ALU.mult),
                       waits=wq + gcol, inc=env.eV)
        lnv = emit_ln_fm_batch(env, src, b, 12, 37, mode='l0')
        for jj in range(NTB):
            for ti, (base_sl, wcol0) in enumerate([(20, 0), (28, 1024)]):
                for nch in range(2):
                    bank = (jj * 2 + nch) % 4
                    ps = env.PS[bank]
                    for kc in range(8):
                        w = []
                        if kc == 0:
                            w = [(env.eV, max(lnv, gev))] + env.ps_wait(bank)
                        sl, col = WQKV_PLACE[kc]
                        lhsT = env.slab(12 + kc, jj * 128, 128)
                        rhs = ap_mk(env.AB, sl * 2048 + col + wcol0 + nch * 512, [[1, 512]])
                        P.op("tensor", lambda e, _o=ps[:], _l=lhsT, _r=rhs, _s=(kc == 0),
                             _p=(kc == 7): e.matmul(_o, _l, _r, start=_s, stop=_p),
                             waits=w, inc=env.eT if kc == 7 else None)
                    tmm = env.eT.cnt
                    ev = P.op("vector", lambda e,
                              _o=env.slab(base_sl + jj // 2, (jj % 2) * 1024 + nch * 512, 512),
                              _i=ps[:]: e.tensor_copy(_o, _i),
                              waits=[(env.eT, tmm)], inc=env.eV)
                    env.ps_done(bank, env.eV, ev)
        qkv_done = env.eV.cnt
        sq_mm_tgt = {}
        for jj in range(NTB):
            w = [(env.eV, qkv_done)] if jj == 0 else []
            if jj >= 2:
                w = w + [(env.eT, sq_mm_tgt[jj - 2])]
            P.op("scalar", lambda e, _o=env.slab(37, (jj % 2) * 1024, 1024),
                 _i=env.slab(28 + jj // 2, (jj % 2) * 1024, 1024):
                 e.activation(_o, _i, AF.Square), waits=w, inc=env.eS)
            sq = env.eS.cnt
            for nch in range(2):
                w = [(env.eS, sq)]
                if jj == 0:
                    w += env.ps_wait(4 + nch)
                P.op("tensor", lambda e, _b=4 + nch, _o=None, _c=nch,
                     _r=env.slab(37, (jj % 2) * 1024 + nch * 512, 512),
                     _s=(jj == 0), _p=(jj == NTB - 1):
                     e.matmul(env.PS[_b][0:1, 0:512], env.cones[:, 0:1], _r,
                              start=_s, stop=_p),
                     waits=w, inc=env.eT)
            sq_mm_tgt[jj] = env.eT.cnt
        knorm_mm = env.eT.cnt
        k1 = None
        for nch in range(2):
            k1 = P.op("scalar", lambda e, _b=4 + nch, _c=nch: e.activation(
                env.row(3, _c * 512, 512), env.PS[_b][0:1, 0:512], AF.Sqrt),
                waits=[(env.eT, knorm_mm)], inc=env.eS)
        def _recip_rk(e):
            with env.nc.allow_low_precision(reason="rk row in bf16 (matmul operand class)"):
                return e.reciprocal(env.row(2, 0, 1024), env.row(3, 0, 1024))
        k2 = P.op("vector", _recip_rk, waits=[(env.eS, k1)], inc=env.eV)
        env.ps_done(4, env.eV, k2)
        env.ps_done(5, env.eV, k2)
        if b == 0 and getattr(env, "debug", False):
            env.debug_dump("ktm0", env.slab(28, 0, 2048), [(env.eV, k2)])
            env.debug_dump("qtm0", env.slab(20, 0, 2048), [(env.eV, k2)])
        for mc in range(8):
            for hh in range(2):
                P.op("vector", lambda e, _mc=mc, _hh=hh: e.tensor_copy(
                    env.AF[_hh * 64:_hh * 64 + 64, 8328 + _mc:8329 + _mc],
                    env.AF[0:64, 8296 + 2 * _mc + _hh:8297 + 2 * _mc + _hh]),
                    waits=[(env.eV, at4)], inc=env.eV)
        aall = env.eV.cnt
        for mc in range(8):
            bank = mc % 2
            ps = env.PS[bank]
            SCp = SC + 160 * (mc % 2)
            simt = env.AF[:, SCp:SCp + 64]
            dcol = env.AF[:, SCp + 64:SCp + 65]
            sqcol = env.AF[:, SCp + 65:SCp + 66]
            rqcol = env.AF[:, SCp + 66:SCp + 67]
            mxcol = env.AF[:, SCp + 67:SCp + 68]
            nbcol = env.AF[:, SCp + 68:SCp + 69]
            dencol = env.AF[:, SCp + 69:SCp + 70]
            prodc = env.AF[:, SCp + 70:SCp + 134]
            acol2 = env.AF[:, 8328 + mc:8329 + mc]
            rdcol = env.AF[:, 8336 + mc % 2 * 8:8337 + mc % 2 * 8]
            rdA = env.AF[0:64, 8344 + 2 * mc:8345 + 2 * mc]
            rdB = env.AF[64:128, 8344 + 2 * mc:8345 + 2 * mc]
            scw = env._l0a_scratch_guard.get(mc % 2)
            for jj in range(NTB):
                w = []
                if jj == 0:
                    w = env.ps_wait(bank) + [(env.eV, k2), (env.eV, aall)]
                lhsT = env.AB[:, (20 + jj // 2) * 2048 + (jj % 2) * 1024 + mc * 128:
                              (20 + jj // 2) * 2048 + (jj % 2) * 1024 + mc * 128 + 128]
                rhs = env.AB[:, (28 + jj // 2) * 2048 + (jj % 2) * 1024 + mc * 128:
                             (28 + jj // 2) * 2048 + (jj % 2) * 1024 + mc * 128 + 128]
                P.op("tensor", lambda e, _o=ps[:, 0:128], _l=lhsT, _r=rhs,
                     _s=(jj == 0), _p=(jj == NTB - 1):
                     e.matmul(_o, _l, _r, start=_s, stop=_p),
                     waits=w, inc=env.eT if (jj == NTB - 1) else None)
            psq = env.PS[2 + mc % 2]
            for jj in range(NTB):
                w = env.ps_wait(2 + mc % 2) if jj == 0 else []
                lhsT = env.AB[:, (20 + jj // 2) * 2048 + (jj % 2) * 1024 + mc * 128:
                              (20 + jj // 2) * 2048 + (jj % 2) * 1024 + mc * 128 + 128]
                P.op("tensor", lambda e, _o=psq[:, 0:128], _l=lhsT,
                     _s=(jj == 0), _p=(jj == NTB - 1):
                     e.matmul(_o, _l, _l, start=_s, stop=_p),
                     waits=w, inc=env.eT if (jj == NTB - 1) else None)
            gmm = env.eT.cnt
            P.op("vector", lambda e, _o=prodc[0:64, :], _i=psq[0:64, 0:64]:
                 e.tensor_tensor(_o, _i, env.ident[0:64, 0:64], ALU.mult),
                 waits=[(env.eT, gmm)] + ([scw] if scw else []), inc=env.eV)
            dq0 = P.op("vector", lambda e, _o=prodc[64:128, :], _i=psq[64:128, 64:128]:
                 e.tensor_tensor(_o, _i, env.ident[64:128, 64:128], ALU.mult),
                 inc=env.eV)
            dq = P.op("vector", lambda e, _o=dcol, _i=prodc: e.reduce_sum(
                _o, _i, axis=AX.X), waits=[(env.eV, dq0)], inc=env.eV)
            sq1 = P.op("scalar", lambda e, _o=sqcol, _i=dcol: e.activation(
                _o, _i, AF.Sqrt), waits=[(env.eV, dq)], inc=env.eS)
            rq = P.op("vector", lambda e, _o=rqcol, _i=sqcol: e.reciprocal(_o, _i),
                      waits=[(env.eS, sq1)], inc=env.eV)
            bkw = env.ps_wait(4)
            P.op("tensor", lambda e, _mc=mc: e.matmul(env.PS[4][0:64, 0:64], env.ones1(64),
                 env.row(2, 2 * _mc * 64, 64), start=True, stop=True), waits=bkw)
            bk = P.op("tensor", lambda e, _mc=mc: e.matmul(env.PS[4][64:128, 0:64],
                      env.ones1(64), env.row(2, (2 * _mc + 1) * 64, 64),
                      start=True, stop=True), inc=env.eT)
            rkb = P.op("vector", lambda e: e.tensor_copy(
                       ap_mk(env.AB, 36 * 2048 + 1024, [[1, 64]]),
                       env.PS[4][:, 0:64]), waits=[(env.eT, bk)], inc=env.eV)
            env.ps_done(4, env.eV, rkb)
            s1_ = P.op("vector", lambda e, _o=simt, _g=ps[0:64, 0:64]: e.tensor_tensor(
                _o[0:64, :], _g,
                ap_mk(env.AB, 36 * 2048 + 1024, [[1, 64]], nparts=64),
                ALU.mult), waits=[(env.eV, rkb)], inc=env.eV)
            sim = P.op("vector", lambda e, _o=simt, _g=ps[64:128, 64:128]: e.tensor_tensor(
                _o[64:128, :], _g,
                ap_mk(env.AB, 36 * 2048 + 1024, [[1, 64]], part0=64, nparts=64),
                ALU.mult), waits=[(env.eV, s1_)], inc=env.eV)
            env.ps_done(bank, env.eV, sim)
            env.ps_done(2 + mc % 2, env.eV, sim)
            if b == 0 and mc == 0 and getattr(env, "debug", False):
                dcp = P.op("vector", lambda e, _g=ps[:, 0:128]: e.tensor_copy(
                    env.slab(38, 0, 128), _g),
                    waits=[(env.eV, sim)], inc=env.eV)
                env.ps_done(bank, env.eV, dcp)
            simq = P.op("vector", lambda e, _o=simt, _r=rqcol: e.tensor_scalar(
                _o, _o, _r, None, ALU.mult), waits=[(env.eV, sim)], inc=env.eV)
            mx = P.op("vector", lambda e, _o=mxcol, _i=simt: e.reduce_max(
                _o, _i, axis=AX.X), waits=[(env.eV, simq)], inc=env.eV)
            nb = P.op("vector", lambda e, _o=nbcol, _i=mxcol, _a=acol2: e.tensor_scalar(
                _o, _i, _a, -1.0, ALU.mult, ALU.mult),
                waits=[(env.eV, mx)], inc=env.eV)
            if b == 0 and mc == 0 and getattr(env, "debug", False):
                P.op("vector", lambda e, _s=simt: e.tensor_copy(env.slab(38, 128, 64), _s),
                     waits=[(env.eV, nb)], inc=env.eV)
            px = P.op("scalar", lambda e, _i=simt, _b=nbcol, _d=dencol, _a=acol2:
                      e.activation(ap_mk(env.AB, 36 * 2048 + 1152, [[1, 64]]),
                                   _i, AF.Exp, bias=_b, scale=_a, accum_out=_d),
                      waits=[(env.eV, nb)], inc=env.eS)
            env._l0a_scratch_guard[mc % 2] = (env.eS, px)
            rd = P.op("vector", lambda e, _o=env.AF[:, 8344 + 2 * mc:8345 + 2 * mc],
                      _i=dencol: e.reciprocal(_o, _i), waits=[(env.eS, px)], inc=env.eV)
            if b == 0 and mc == 0 and getattr(env, "debug", False):
                P.op("vector", lambda e: e.tensor_copy(
                    env.slab(38, 192, 64), ap_mk(env.AB, 36 * 2048 + 1152, [[1, 64]])),
                    waits=[(env.eS, px)], inc=env.eV)
            tr = P.op("tensor", lambda e: e.transpose(env.PT[0][0:64, 0:128],
                      ap_mk(env.AB, 36 * 2048 + 1152, [[1, 64]]),
                      env.ident[:]),
                      waits=[(env.eS, px)] + env.pt_wait(0), inc=env.eT)
            teA = P.op("vector", lambda e, _mc=mc: e.tensor_copy(
                      ap_mk(env.AB, 36 * 2048 + 2 * _mc * 64, [[1, 64]], nparts=64),
                      env.PT[0][0:64, 0:64]), waits=[(env.eT, tr), (env.eV, rd)],
                      inc=env.eV)
            teB = P.op("vector", lambda e, _mc=mc: e.tensor_copy(
                      ap_mk(env.AB, 36 * 2048 + (2 * _mc + 1) * 64, [[1, 64]],
                            part0=64, nparts=64),
                      env.PT[0][0:64, 64:128]), waits=[(env.eV, teA)], inc=env.eV)
            env.pt_done(0, env.eV, teB)
        pt_all = env.eV.cnt
        for mc in range(8):
            for nch in range(4):
                bank = (mc * 4 + nch) % 4
                ps = env.PS[bank]
                for kc in range(8):
                    w = []
                    if kc == 0:
                        w = [(env.eV, pt_all)] + env.ps_wait(bank)
                    sl, col = WQKV_PLACE[kc]
                    lhsT = ap_mk(env.AB, sl * 2048 + col + 2048 + mc * 128, [[1, 128]])
                    rhs = env.slab(12 + kc, nch * 512, 512)
                    P.op("tensor", lambda e, _o=ps[:], _l=lhsT, _r=rhs, _s=(kc == 0),
                         _p=(kc == 7): e.matmul(_o, _l, _r, start=_s, stop=_p),
                         waits=w, inc=env.eT if kc == 7 else None)
                tmm = env.eT.cnt
                ev = P.op("vector", lambda e, _o=env.slab(20 + mc, nch * 512, 512),
                          _i=ps[:]: e.tensor_copy(_o, _i), waits=[(env.eT, tmm)], inc=env.eV)
                env.ps_done(bank, env.eV, ev)
        v_done = env.eV.cnt
        for h in range(16):
            for nch in range(4):
                bank = (h * 4 + nch) % 4
                ps = env.PS[bank]
                w = [(env.eV, v_done)] + env.ps_wait(bank)
                lhsT = ap_mk(env.AB, 36 * 2048 + h * 64, [[1, 64]],
                             part0=(h % 2) * 64, nparts=64)
                rhs = ap_mk(env.AB, (20 + h // 2) * 2048 + nch * 512, [[1, 512]],
                            part0=(h % 2) * 64, nparts=64)
                mm = P.op("tensor", lambda e, _o=ps[0:64, :], _l=lhsT, _r=rhs:
                          e.matmul(_o, _l, _r, start=True, stop=True),
                          waits=w, inc=env.eT)
                ovap = ap_mk(env.AB, (28 + h // 2) * 2048 + nch * 512, [[1, 512]],
                             part0=(h % 2) * 64, nparts=64)
                ev = P.op("vector", lambda e, _h=h, _o=ovap, _i=ps[0:64, :]:
                          e.tensor_scalar(
                              _o, _i,
                              env.AF[(_h % 2) * 64:(_h % 2) * 64 + 64,
                                     8344 + 2 * (_h // 2):8345 + 2 * (_h // 2)],
                              None, ALU.mult),
                          waits=[(env.eT, mm)], inc=env.eV)
                env.ps_done(bank, env.eV, ev)
        o_done = env.eV.cnt
        if b == 0 and getattr(env, "debug", False):
            env.debug_dump("dbgslab", env.slab(38, 0, 512), [(env.eV, o_done)])
        P.wait("gpsimd", [(env.eT, env.eT.cnt)])
        wo = load_weight_bf16(env, wp + "wout", [(r // 2, (r % 2) * 1024) for r in range(8)])
        for jj in range(NTB):
            j = b * NTB + jj
            tgts = []
            for nch in range(2):
                bank = (jj % 2) * 2 + nch
                ps = env.PS[bank]
                for kc in range(8):
                    w = []
                    if kc == 0:
                        w = [(env.eV, o_done)] + wo + env.ps_wait(bank)
                    lhsT = env.slab(28 + kc, jj * 128, 128)
                    rhs = ap_mk(env.AB, (kc // 2) * 2048 + (kc % 2) * 1024 + nch * 512,
                                [[1, 512]])
                    P.op("tensor", lambda e, _o=ps[:], _l=lhsT, _r=rhs, _s=(kc == 0),
                         _p=(kc == 7): e.matmul(_o, _l, _r, start=_s, stop=_p),
                         waits=w, inc=env.eT if kc == 7 else None)
                tgts.append(env.eT.cnt)
            st, tv = emit_residual_store(
                env, src, dst, j,
                [env.PS[(jj % 2) * 2][:], env.PS[(jj % 2) * 2 + 1][:]],
                [[(env.eT, tgts[0])], [(env.eT, tgts[1])]])
            env.ps_done((jj % 2) * 2, env.eV, tv)
            env.ps_done((jj % 2) * 2 + 1, env.eV, tv)


WQKV_PLACE = [(0, 0), (1, 1024), (3, 0), (4, 1024), (6, 0), (7, 1024), (9, 0), (10, 1024)]


def prep_l0a_make(pfx_in, pfx_out):
    def prep(inputs, core, dram):
        dram[pfx_out + "wqkv"] = np.ascontiguousarray(
            np.asarray(inputs[pfx_in + "w_qkv"], np.float32))
        dram[pfx_out + "wout"] = np.ascontiguousarray(
            np.asarray(inputs[pfx_in + "w_out"], np.float32))
        dram[pfx_out + "gammacol"] = np.ascontiguousarray(
            np.asarray(inputs[pfx_in + "gamma"], np.float32).reshape(8, 128).T)
        dram[pfx_out + "temp"] = np.ascontiguousarray(
            np.asarray(inputs[pfx_in + "temperature"], np.float32).reshape(1, 16))
    return prep




WPL = [(32 + r // 2, (r % 2) * 1024) for r in range(8)]


def load_W(env, name):
    env.P.wait("gpsimd", [(env.eT, env.eT.cnt)])
    return load_weight_bf16(env, name, WPL)


def wslice(env, kc, cols0, n):
    sl, col = WPL[kc]
    return ap_mk(env.AB, sl * 2048 + col + cols0, [[1, n]])


def produce_fm(env, out_slab0, wwait, lnv, scale=None, bias_fn=None, xn_slab0=0):
    P = env.P
    for oc in range(8):
        for nch in range(4):
            bank = (oc * 4 + nch) % 4
            ps = env.PS[bank]
            for kc in range(8):
                w = []
                if kc == 0:
                    w = ([(env.eV, lnv)] if lnv else []) + wwait + env.ps_wait(bank)
                P.op("tensor", lambda e, _o=ps[:], _l=wslice(env, kc, oc * 128, 128),
                     _r=env.slab(xn_slab0 + kc, nch * 512, 512), _s=(kc == 0),
                     _p=(kc == 7): e.matmul(_o, _l, _r, start=_s, stop=_p),
                     waits=w, inc=env.eT if kc == 7 else None)
            tmm = env.eT.cnt
            bias = bias_fn(oc) if bias_fn else None
            if bias is None and scale is None:
                ev = P.op("vector", lambda e, _o=env.slab(out_slab0 + oc, nch * 512, 512),
                          _i=ps[:]: e.tensor_copy(_o, _i),
                          waits=[(env.eT, tmm)], inc=env.eV)
            elif bias is None:
                ev = P.op("vector", lambda e, _o=env.slab(out_slab0 + oc, nch * 512, 512),
                          _i=ps[:], _s=scale: e.tensor_scalar(_o, _i, _s, None, ALU.mult),
                          waits=[(env.eT, tmm)], inc=env.eV)
            elif scale is None:
                ev = P.op("vector", lambda e, _o=env.slab(out_slab0 + oc, nch * 512, 512),
                          _i=ps[:], _b=bias: e.tensor_scalar(_o, _i, _b, None, ALU.add),
                          waits=[(env.eT, tmm)], inc=env.eV)
            else:
                ev = P.op("vector", lambda e, _o=env.slab(out_slab0 + oc, nch * 512, 512),
                          _i=ps[:], _b=bias, _s=scale:
                          e.tensor_scalar(_o, _i, _b, _s, ALU.add, ALU.mult),
                          waits=[(env.eT, tmm)], inc=env.eV)
            env.ps_done(bank, env.eV, ev)
    return env.eV.cnt


def produce_tm(env, out_slab0, wwait, lnv, bias_row=None, xn_slab0=0):
    P = env.P
    for jj in range(NTB):
        for nch in range(2):
            bank = (jj * 2 + nch) % 4
            ps = env.PS[bank]
            for kc in range(8):
                w = []
                if kc == 0:
                    w = ([(env.eV, lnv)] if lnv else []) + wwait + env.ps_wait(bank)
                lhsT = env.slab(xn_slab0 + kc, jj * 128, 128)
                P.op("tensor", lambda e, _o=ps[:], _l=lhsT,
                     _r=wslice(env, kc, nch * 512, 512), _s=(kc == 0),
                     _p=(kc == 7 and bias_row is None):
                     e.matmul(_o, _l, _r, start=_s, stop=_p),
                     waits=w, inc=env.eT if (kc == 7 and bias_row is None) else None)
            if bias_row is not None:
                P.op("tensor", lambda e, _o=ps[:], _l=env.ones1(128),
                     _r=bias_row(nch): e.matmul(_o, _l, _r, start=False, stop=True),
                     inc=env.eT)
            tmm = env.eT.cnt
            ev = P.op("vector", lambda e, _o=env.slab(out_slab0 + jj // 2,
                      (jj % 2) * 1024 + nch * 512, 512),
                      _i=ps[:]: e.tensor_copy(_o, _i),
                      waits=[(env.eT, tmm)], inc=env.eV)
            env.ps_done(bank, env.eV, ev)
    return env.eV.cnt


def emit_sm_attn(env, b, h, jt, qkv_done, q_slab0=8, k_slab0=16, v_slab0=24,
                 o_slab0=0, p_slab=36, pt_slab=37, extra_score=None, nkc=16,
                 o_dram=None):
    P = env.P
    par = (h * NTB + jt) % 2
    negm = env.AF[:, 8448 + par * 8:8449 + par * 8]
    den = env.AF[:, 8450 + par * 8:8451 + par * 8]
    rden = env.AF[:, 8452 + par * 8:8453 + par * 8]
    hq = (h % 2) * 64
    for kc in range(4):
        w = env.ps_wait(kc) if True else []
        if kc == 0:
            w = w + [(env.eV, qkv_done)]
        lhsT = ap_mk(env.AB, (q_slab0 + h // 2) * 2048 + jt * 128, [[1, 128]],
                     part0=hq, nparts=64)
        rhs = ap_mk(env.AB, (k_slab0 + h // 2) * 2048 + kc * 512, [[1, 512]],
                    part0=hq, nparts=64)
        last = extra_score is None
        P.op("tensor", lambda e, _o=env.PS[kc][:], _l=lhsT, _r=rhs, _p=last:
             e.matmul(_o, _l, _r, start=True, stop=_p),
             waits=w, inc=env.eT if last else None)
        if extra_score is not None:
            extra_score(env, h, jt, kc)
    smm = env.eT.cnt
    sg = env._sm_guard.get(par)
    mx = P.op("vector", lambda e, _o=negm: e.reduce_max(
        _o, env.PSB[:, 0:2048], axis=AX.X, negate=True),
        waits=[(env.eT, smm)] + ([sg] if sg else []), inc=env.eV)
    px = P.op("scalar", lambda e, _b=negm, _d=den: e.activation(
        env.slab(p_slab, 0, 2048), env.PSB[:, 0:2048], AF.Exp, bias=_b,
        accum_out=_d), waits=[(env.eV, mx)], inc=env.eS)
    env._sm_guard[par] = (env.eS, px)
    for kc in range(4):
        env.ps_done(kc, env.eS, px)
    rd = P.op("vector", lambda e, _o=rden, _i=den: e.reciprocal(_o, _i),
              waits=[(env.eS, px)], inc=env.eV)
    for half in range(2):
        pt = env.PT[half]
        for k8 in range(8):
            kc = half * 8 + k8
            w = []
            if k8 == 0:
                w = [(env.eS, px)] + env.pt_wait(half)
            P.op("tensor", lambda e, _o=pt[:, k8 * 128:(k8 + 1) * 128],
                 _i=env.slab(p_slab, kc * 128, 128):
                 e.transpose(_o, _i, env.ident[:]), waits=w,
                 inc=env.eT if k8 == 7 else None)
        tT = env.eT.cnt
        tev = P.op("vector", lambda e, _o=env.slab(pt_slab, half * 1024, 1024),
                   _i=pt[:, 0:1024]: e.tensor_copy(_o, _i),
                   waits=[(env.eT, tT)], inc=env.eV)
        env.pt_done(half, env.eV, tev)
    pt_done = env.eV.cnt
    for kc in range(nkc):
        w = []
        if kc == 0:
            w = [(env.eV, pt_done)] + env.ps_wait(4)
        lhsT = env.slab(pt_slab, kc * 128, 128)
        rhs = ap_mk(env.AB, (v_slab0 + kc // 2) * 2048 + (kc % 2) * 1024 + h * 64,
                    [[1, 64]])
        P.op("tensor", lambda e, _o=env.PS[4][:, 0:64], _l=lhsT, _r=rhs,
             _s=(kc == 0), _p=(kc == nkc - 1):
             e.matmul(_o, _l, _r, start=_s, stop=_p),
             waits=w, inc=env.eT if kc == nkc - 1 else None)
    pvm = env.eT.cnt
    if o_dram is not None:
        slot = (h * NTB + jt) % 3
        g = env._af_guard.get(slot)
        stg = env.AF[:, 8528 + 64 * slot:8592 + 64 * slot]
        ov = P.op("vector", lambda e, _o=stg, _i=env.PS[4][:, 0:64], _r=rden:
                  e.tensor_scalar(_o, _i, _r, None, ALU.mult),
                  waits=[(env.eT, pvm), (env.eV, rd)] + ([g] if g else []),
                  inc=env.eV)
        env.ps_done(4, env.eV, ov)
        st = env.dma("sync", lambda e, _o=o_dram[jt * 128:(jt + 1) * 128,
                     h * 64:(h + 1) * 64], _i=stg:
                     e.dma_start(out=_o, in_=_i), waits=[(env.eV, ov)],
                     slot=("f", slot))
        env._af_guard[slot] = st
        return ov
    ov = P.op("vector", lambda e, _o=ap_mk(env.AB, (o_slab0 + jt // 2) * 2048 +
              (jt % 2) * 1024 + h * 64, [[1, 64]]), _i=env.PS[4][:, 0:64], _r=rden:
              e.tensor_scalar(_o, _i, _r, None, ALU.mult),
              waits=[(env.eT, pvm), (env.eV, rd)], inc=env.eV)
    env.ps_done(4, env.eV, ov)
    return ov


def emit_rope_tm(env, tile_base_col, jt, ctab, stab, wait_ev, strided_in=False,
                 scratch_col=None):
    P = env.P
    if strided_in:
        X1 = ap_mk(env.AB, tile_base_col, [[64, 16], [2, 32]])
        X2 = ap_mk(env.AB, tile_base_col + 1, [[64, 16], [2, 32]])
    else:
        X1 = ap_mk(env.AB, tile_base_col, [[64, 16], [1, 32]])
        X2 = ap_mk(env.AB, tile_base_col + 32, [[64, 16], [1, 32]])
    X1o = ap_mk(env.AB, tile_base_col, [[64, 16], [1, 32]])
    X2o = ap_mk(env.AB, tile_base_col + 32, [[64, 16], [1, 32]])
    sc = scratch_col if scratch_col is not None else 36 * 2048
    C = ap_mk(env.AB, ctab + 32 * jt, [[0, 16], [1, 32]])
    S = ap_mk(env.AB, stab + 32 * jt, [[0, 16], [1, 32]])
    t = []
    for i, (xv, tv) in enumerate([(X1, C), (X2, S), (X1, S), (X2, C)]):
        t.append(P.op("vector", lambda e, _o=ap_mk(env.AB, sc + i * 512, [[1, 512]]),
                      _a=xv, _b=tv: e.tensor_tensor(_o, _a, _b, ALU.mult),
                      waits=list(wait_ev) if i == 0 else [], inc=env.eV))
    P.op("vector", lambda e, _o=X1o, _a=ap_mk(env.AB, sc, [[1, 512]]),
         _b=ap_mk(env.AB, sc + 512, [[1, 512]]):
         e.tensor_tensor(_o, _a, _b, ALU.subtract), waits=[(env.eV, t[3])], inc=env.eV)
    last = P.op("vector", lambda e, _o=X2o, _a=ap_mk(env.AB, sc + 1024, [[1, 512]]),
                _b=ap_mk(env.AB, sc + 1536, [[1, 512]]):
                e.tensor_tensor(_o, _a, _b, ALU.add), inc=env.eV)
    return last


def produce_tm_rope_fm(env, out_fm_slab0, wwait, lnv, scale, bias_row,
                       ctab, stab, xn_slab0=0):
    P = env.P
    ev = None
    for jj in range(NTB):
        for nch in range(2):
            bank = (jj * 2 + nch) % 4
            ps = env.PS[bank]
            for kc in range(8):
                w = []
                if kc == 0:
                    w = ([(env.eV, lnv)] if lnv else []) + wwait + env.ps_wait(bank)
                lhsT = env.slab(xn_slab0 + kc, jj * 128, 128)
                P.op("tensor", lambda e, _o=ps[:], _l=lhsT,
                     _r=wslice(env, kc, nch * 512, 512), _s=(kc == 0):
                     e.matmul(_o, _l, _r, start=_s, stop=False), waits=w)
            P.op("tensor", lambda e, _o=ps[:], _l=env.ones1(128),
                 _r=bias_row(nch): e.matmul(_o, _l, _r, start=False, stop=True),
                 inc=env.eT)
            tmm = env.eT.cnt
            if scale is None:
                evv = P.op("vector", lambda e, _o=env.slab(36, nch * 512, 512),
                           _i=ps[:]: e.tensor_copy(_o, _i),
                           waits=[(env.eT, tmm)], inc=env.eV)
            else:
                evv = P.op("vector", lambda e, _o=env.slab(36, nch * 512, 512),
                           _i=ps[:], _s=scale: e.tensor_scalar(_o, _i, _s, None, ALU.mult),
                           waits=[(env.eT, tmm)], inc=env.eV)
            env.ps_done(bank, env.eV, evv)
        rr = emit_rope_tm(env, 36 * 2048, jj, ctab, stab, [(env.eV, env.eV.cnt)],
                          scratch_col=37 * 2048)
        dstap = ap_mk(env.AB, out_fm_slab0 * 2048 + jj * 128, [[2048, 8], [1, 128]])
        ev = emit_transpose_8(env, env.slab(36, 0, 1024), dstap, [(env.eV, rr)],
                              pt_idx=jj % 2)
    return ev


def emit_l1a(env, src, dst, wp):
    P = env.P
    bqr = load_row(env, wp + "bqrow", 1)
    bkr = load_row(env, wp + "bkrow", 3)
    bvr = load_row(env, wp + "bvrow", 0)
    bor = load_row(env, wp + "borow", 2)
    tmt = load_weight_bf16(env, wp + "tmtables", [(38, 0)])
    for b in range(B):
        lnv = emit_ln_fm_batch(env, src, b, 0, 37)
        wq = load_W(env, wp + "WqT")
        produce_tm_rope_fm(env, 8, wq + bqr + tmt, lnv, 0.125,
                           lambda nch: env.row(1, nch * 512, 512),
                           38 * 2048, 38 * 2048 + 512)
        wk = load_W(env, wp + "WkT")
        produce_tm_rope_fm(env, 16, wk + bkr, None, None,
                           lambda nch: env.row(3, nch * 512, 512),
                           38 * 2048, 38 * 2048 + 512)
        wv = load_W(env, wp + "WvT")
        vv = produce_tm(env, 24, wv + bvr + tmt, None,
                        bias_row=lambda nch: env.row(0, nch * 512, 512))
        vr = None
        for jt in range(NTB):
            w0 = [(env.eV, vv if jt == 0 else vr)]
            if jt == 0:
                w0 = w0 + tmt
            vr = emit_rope_tm(env, (24 + jt // 2) * 2048 + (jt % 2) * 1024, jt,
                              38 * 2048, 38 * 2048 + 512, w0)
        qkv_done = env.eV.cnt
        env._sm_guard = {}
        for h in range(16):
            for jt in range(NTB):
                emit_sm_attn(env, b, h, jt, qkv_done)
        orp = env.eV.cnt
        for jt in range(NTB):
            orp = emit_rope_tm(env, (jt // 2) * 2048 + (jt % 2) * 1024, jt,
                               38 * 2048 + 1024, 38 * 2048 + 1536,
                               [(env.eV, orp)], strided_in=True)
        ofm = None
        for jt in range(NTB):
            dstap = ap_mk(env.AB, 16 * 2048 + jt * 128, [[2048, 8], [1, 128]])
            ofm = emit_transpose_8(env, env.slab(jt // 2, (jt % 2) * 1024, 1024),
                                   dstap, [(env.eV, orp)], pt_idx=jt % 2)
        wo = load_W(env, wp + "WoT")
        for jj in range(NTB):
            j = b * NTB + jj
            tgts = []
            for nch in range(2):
                bank = (jj % 2) * 2 + nch
                ps = env.PS[bank]
                for kc in range(8):
                    w = []
                    if kc == 0:
                        w = [(env.eV, ofm)] + wo + bor + env.ps_wait(bank)
                    lhsT = env.slab(16 + kc, jj * 128, 128)
                    P.op("tensor", lambda e, _o=ps[:], _l=lhsT,
                         _r=wslice(env, kc, nch * 512, 512), _s=(kc == 0):
                         e.matmul(_o, _l, _r, start=_s, stop=False),
                         waits=w)
                P.op("tensor", lambda e, _o=ps[:], _l=env.ones1(128),
                     _r=env.row(2, nch * 512, 512):
                     e.matmul(_o, _l, _r, start=False, stop=True), inc=env.eT)
                tgts.append(env.eT.cnt)
            st, tv = emit_residual_store(
                env, src, dst, j,
                [env.PS[(jj % 2) * 2][:], env.PS[(jj % 2) * 2 + 1][:]],
                [[(env.eT, tgts[0])], [(env.eT, tgts[1])]])
            env.ps_done((jj % 2) * 2, env.eV, tv)
            env.ps_done((jj % 2) * 2 + 1, env.eV, tv)


def prep_l1a(inputs, core, dram):
    idx64 = np.concatenate([np.arange(0, 64, 2), np.arange(1, 64, 2)])
    perm = np.concatenate([64 * h + idx64 for h in range(16)])
    Wq = np.asarray(inputs["L1_attn_a_Wq"], np.float32)
    Wk = np.asarray(inputs["L1_attn_a_Wk"], np.float32)
    Wv = np.asarray(inputs["L1_attn_a_Wv"], np.float32)
    Wo = np.asarray(inputs["L1_attn_a_Wo"], np.float32)
    dram["l1a_WqT"] = np.ascontiguousarray(Wq.T[:, perm])
    dram["l1a_WkT"] = np.ascontiguousarray(Wk.T[:, perm])
    dram["l1a_WvT"] = np.ascontiguousarray(Wv.T[:, perm])
    dram["l1a_WoT"] = np.ascontiguousarray(Wo.T)
    dram["l1a_bqrow"] = np.ascontiguousarray(
        np.asarray(inputs["L1_attn_a_bq"], np.float32)[perm].reshape(1, 1024))
    dram["l1a_bkrow"] = np.ascontiguousarray(
        np.asarray(inputs["L1_attn_a_bk"], np.float32)[perm].reshape(1, 1024))
    dram["l1a_bvrow"] = np.ascontiguousarray(
        np.asarray(inputs["L1_attn_a_bv"], np.float32)[perm].reshape(1, 1024))
    dram["l1a_borow"] = np.ascontiguousarray(
        np.asarray(inputs["L1_attn_a_bo"], np.float32).reshape(1, 1024))
    pos = np.arange(2048, dtype=np.float64)[:, None]
    dp = np.arange(0, 64, 2, dtype=np.float64)[None, :]
    theta = 10000.0 ** (-dp / 64.0)
    fr = pos * theta
    ropemat = np.concatenate([np.cos(fr), np.sin(fr)], axis=1)
    c_vec = ropemat[:, 0::2]
    s_vec = ropemat[:, 1::2]
    ctm = np.zeros((128, 512), np.float64)
    stm = np.zeros((128, 512), np.float64)
    for jt in range(16):
        ctm[:, 32 * jt:32 * jt + 32] = c_vec[128 * jt:128 * (jt + 1), :]
        stm[:, 32 * jt:32 * jt + 32] = s_vec[128 * jt:128 * (jt + 1), :]
    tmt = np.concatenate([ctm, stm, -ctm, -stm], axis=1)
    dram["l1a_tmtables"] = tmt.astype(BF)



def emit_l1b(env, src, dst, wp):
    P = env.P
    bq = load_bias_cols(env, wp + "bqcol", 8256, 8)
    bk = load_bias_cols(env, wp + "bkcol", 8264, 8)
    bv = load_bias_cols(env, wp + "bvcol", 8272, 8)
    bqs = load_bias_cols(env, wp + "bqscol", 8280, 1)
    bvr = load_row(env, wp + "bvrow", 0)
    bor = load_row(env, wp + "borow", 2)
    tcl = load_bias_cols(env, wp + "tempcol", 8441, 1)
    tempcol = env.AF[:, 8441:8442]
    tb = P.op("vector", lambda e: e.tensor_scalar(env.AF[:, 8281:8282],
              env.AF[:, 8280:8281], tempcol[:, 0:1], None, ALU.mult),
              waits=tcl + bqs, inc=env.eV)
    tsbqs = env.AF[:, 8281:8282]
    wsd = load_weight_bf16(env, wp + "wsdup", [(38, 0)])
    wsb = load_weight_bf16(env, wp + "wsbias", [(38, 16)])

    def ws3(idx, base):
        return ap_mk(env.AB, 38 * 2048 + 3 * idx, [[1, 3]], part0=base, nparts=64)

    def bias3(idx):
        return ap_mk(env.AB, 38 * 2048 + 16 + 3 * idx, [[1, 3]], nparts=1)

    osc = env.nc.dram_tensor("l1b_osc", [2048, 1024], F32, kind="Internal").ap()
    dbg = getattr(env, "dbg_l1b", None)
    for b in range(B):
        if dbg is not None and b not in dbg["batches"]:
            continue
        lnv = emit_ln_fm_batch(env, src, b, 0, 37)
        wq = load_W(env, wp + "WqT")
        produce_fm(env, 8, wq + bq + wsd + wsb + [(env.eV, tb)], lnv, scale=tempcol,
                   bias_fn=lambda oc: env.AF[:, 8256 + oc:8257 + oc])
        wk = load_W(env, wp + "WkT")
        produce_fm(env, 16, wk + bk, None,
                   bias_fn=lambda oc: env.AF[:, 8264 + oc:8265 + oc])
        wv = load_W(env, wp + "WvT")
        qkv_done = produce_tm(env, 24, wv + bvr, None,
                              bias_row=lambda nch: env.row(0, nch * 512, 512))
        env._sm_guard = {}
        P.wait("sync", env.qg_now())
        for mc in range(8):
            if dbg is not None and mc > dbg["mc"]:
                break
            P.wait("gpsimd", [(env.eT, env.eT.cnt)])
            wvs = load_weight_bf16(env, wp + "WvTc%d" % mc,
                                   [(38, 32 + 128 * k) for k in range(8)])
            for nch in range(4):
                ps = env.PS[nch]
                for kc in range(8):
                    w = []
                    if kc == 0:
                        w = [(env.eV, qkv_done)] + wvs + bv + env.ps_wait(nch)
                    P.op("tensor", lambda e, _o=ps[:],
                         _l=ap_mk(env.AB, 38 * 2048 + 32 + 128 * kc, [[1, 128]]),
                         _r=env.slab(kc, nch * 512, 512), _s=(kc == 0), _p=(kc == 7):
                         e.matmul(_o, _l, _r, start=_s, stop=_p),
                         waits=w, inc=env.eT if kc == 7 else None)
                tmm = env.eT.cnt
                ev = P.op("vector", lambda e, _o=env.slab(36, nch * 512, 512),
                          _i=ps[:], _b=env.AF[:, 8272 + mc:8273 + mc]:
                          e.tensor_scalar(_o, _i, _b, None, ALU.add),
                          waits=[(env.eT, tmm)], inc=env.eV)
                env.ps_done(nch, env.eV, ev)
            vfm = env.eV.cnt
            for hh in range(2):
                rb = 32 * hh
                vb = hh * 64
                jobs = [(36, 4, 3, 34, rb),
                        (36, 3, 2, 34, 64 + rb),
                        (16 + mc, 1, 0, 35, rb),
                        (16 + mc, 2, 1, 35, 64 + rb),
                        (8 + mc, 0, None, 32, rb)]
                for (ssl, wi, bi, dsl, db) in jobs:
                    for kc in range(4):
                        w = env.ps_wait(4)
                        if kc == 0 and ssl == 36:
                            w = w + [(env.eV, vfm)]
                        rhs = ap_mk(env.AB, ssl * 2048 + kc * 512, [[1, 512]],
                                    part0=vb, nparts=64)
                        P.op("tensor", lambda e, _o=env.PS[4][0:3, 0:512],
                             _l=ws3(wi, vb), _r=rhs, _p=(bi is None):
                             e.matmul(_o, _l, _r, start=True, stop=_p),
                             waits=w, inc=env.eT if bi is None else None)
                        if bi is not None:
                            P.op("tensor", lambda e, _o=env.PS[4][0:3, 0:512],
                                 _l=bias3(bi), _r=env.rows[0:1, 0:512]:
                                 e.matmul(_o, _l, _r, start=False, stop=True),
                                 inc=env.eT)
                        tmm = env.eT.cnt
                        dstap = ap_mk(env.AB, dsl * 2048 + kc * 512, [[1, 512]],
                                      part0=db, nparts=3)
                        if wi == 0:
                            ev = P.op("vector", lambda e, _o=dstap,
                                      _i=env.PS[4][0:3, 0:512],
                                      _b=tsbqs[rb:rb + 3, 0:1]:
                                      e.tensor_scalar(_o, _i, _b, None, ALU.add),
                                      waits=[(env.eT, tmm)], inc=env.eV)
                        else:
                            ev = P.op("vector", lambda e, _o=dstap,
                                      _i=env.PS[4][0:3, 0:512]:
                                      e.tensor_copy(_o, _i),
                                      waits=[(env.eT, tmm)], inc=env.eV)
                        env.ps_done(4, env.eV, ev)
                sall = env.eV.cnt
                a35 = ap_mk(env.AB, 35 * 2048, [[1, 2048]], part0=rb, nparts=3)
                a35h = ap_mk(env.AB, 35 * 2048, [[1, 2048]], part0=64 + rb, nparts=3)
                a34 = ap_mk(env.AB, 34 * 2048, [[1, 2048]], part0=rb, nparts=3)
                a34h = ap_mk(env.AB, 34 * 2048, [[1, 2048]], part0=64 + rb, nparts=3)
                a33 = ap_mk(env.AB, 33 * 2048, [[1, 2048]], part0=rb, nparts=3)
                c1 = P.op("vector", lambda e, _o=a35, _a=a35, _b=a34:
                          e.tensor_tensor(_o, _a, _b, ALU.mult),
                          waits=[(env.eV, sall)], inc=env.eV)
                c2 = P.op("vector", lambda e, _o=a34, _a=a35h, _b=a34h:
                          e.tensor_tensor(_o, _a, _b, ALU.mult),
                          waits=[(env.eV, c1)], inc=env.eV)
                c3 = P.op("vector", lambda e, _o=a33, _a=a35, _b=a34:
                          e.tensor_tensor(_o, _a, _b, ALU.subtract),
                          waits=[(env.eV, c2)], inc=env.eV)
            stp_done = env.eV.cnt

            def extra_score(env_, h_, jt_, kc_, _sd=stp_done):
                rb_ = 32 * (h_ % 2)
                lhsT = ap_mk(env_.AB, 32 * 2048 + jt_ * 128, [[1, 128]],
                             part0=rb_, nparts=3)
                rhs = ap_mk(env_.AB, 33 * 2048 + kc_ * 512, [[1, 512]],
                            part0=rb_, nparts=3)
                env_.P.op("tensor", lambda e, _o=env_.PS[kc_][:], _l=lhsT, _r=rhs:
                          e.matmul(_o, _l, _r, start=False, stop=True),
                          waits=[(env_.eV, _sd)] if (jt_ == 0 and kc_ == 0) else [],
                          inc=env_.eT)

            for hh in range(2):
                h = 2 * mc + hh
                for jt in range(NTB):
                    emit_sm_attn(env, b, h, jt, qkv_done, extra_score=extra_score,
                                 o_dram=osc)
        if dbg is not None:
            return
        P.wait("gpsimd", [(env.eT, env.eT.cnt), (env.eV, env.eV.cnt)])
        P.wait("gpsimd", [(s, s.cnt) for s in env.qF if s.cnt > 0])
        for jj in range(NTB):
            dstap = ap_mk(env.AB, (jj // 2) * 2048 + (jj % 2) * 1024, [[1, 1024]])
            env.dma("gpsimd", lambda e, _d=dstap, _s=osc[jj * 128:(jj + 1) * 128, :]:
                    e.dma_start(out=_d, in_=_s))
        orel = env.qg_now()
        ofm = None
        for jt in range(NTB):
            dstap = ap_mk(env.AB, 16 * 2048 + jt * 128, [[2048, 8], [1, 128]])
            ofm = emit_transpose_8(env, env.slab(jt // 2, (jt % 2) * 1024, 1024),
                                   dstap, list(orel), pt_idx=jt % 2)
        wo = load_W(env, wp + "WoT")
        for jj in range(NTB):
            j = b * NTB + jj
            tgts = []
            for nch in range(2):
                bank = (jj % 2) * 2 + nch
                ps = env.PS[bank]
                for kc in range(8):
                    w = []
                    if kc == 0:
                        w = [(env.eV, ofm)] + wo + bor + env.ps_wait(bank)
                    lhsT = env.slab(16 + kc, jj * 128, 128)
                    P.op("tensor", lambda e, _o=ps[:], _l=lhsT,
                         _r=wslice(env, kc, nch * 512, 512), _s=(kc == 0):
                         e.matmul(_o, _l, _r, start=_s, stop=False),
                         waits=w)
                P.op("tensor", lambda e, _o=ps[:], _l=env.ones1(128),
                     _r=env.row(2, nch * 512, 512):
                     e.matmul(_o, _l, _r, start=False, stop=True), inc=env.eT)
                tgts.append(env.eT.cnt)
            st, tv = emit_residual_store(
                env, src, dst, j,
                [env.PS[(jj % 2) * 2][:], env.PS[(jj % 2) * 2 + 1][:]],
                [[(env.eT, tgts[0])], [(env.eT, tgts[1])]])
            env.ps_done((jj % 2) * 2, env.eV, tv)
            env.ps_done((jj % 2) * 2 + 1, env.eV, tv)


def prep_l1b(inputs, core, dram):
    g = lambda n: np.asarray(inputs["L1_attn_b_" + n], np.float32)
    for nm in ("Wq", "Wk", "Wv", "Wo"):
        dram["l1b_" + nm + "T"] = np.ascontiguousarray(g(nm).T)
    WvT = g("Wv").T
    for mc in range(8):
        dram["l1b_WvTc%d" % mc] = np.ascontiguousarray(WvT[:, mc * 128:(mc + 1) * 128])
    for nm, col in (("bq", "bqcol"), ("bk", "bkcol"), ("bv", "bvcol")):
        dram["l1b_" + col] = np.ascontiguousarray(g(nm).reshape(8, 128).T)
    dram["l1b_bvrow"] = np.ascontiguousarray(g("bv").reshape(1, 1024))
    dram["l1b_borow"] = np.ascontiguousarray(g("bo").reshape(1, 1024))
    dram["l1b_tempcol"] = np.full((128, 1), float(inputs["L1_attn_b_temp"]), np.float32)
    bqsc = np.zeros((128, 1), np.float32)
    bqsc[0:3, 0] = g("bqs")
    bqsc[32:35, 0] = g("bqs")
    dram["l1b_bqscol"] = bqsc
    Wqs, Wks, Wvs = g("Wqs"), g("Wks"), g("Wvs")
    X = np.hstack([Wqs.T, Wks[[1, 2, 0]].T, Wks[[2, 0, 1]].T,
                   Wvs[[1, 2, 0]].T, Wvs[[2, 0, 1]].T])
    dram["l1b_wsdup"] = np.ascontiguousarray(np.vstack([X, X]))
    dram["l1b_wsbias"] = np.ascontiguousarray(np.concatenate(
        [g("bks")[[1, 2, 0]], g("bks")[[2, 0, 1]],
         g("bvs")[[1, 2, 0]], g("bvs")[[2, 0, 1]]]).reshape(1, 12))


def emit_l2a(env, src, dst, wp):
    P = env.P
    P.wait("sync", env.qs_now())
    env.dma("sync", lambda e: e.dma_start(out=env.AF[0:1, 8260:8261],
            in_=env.dram[wp + "xsa"]))
    xw = env.qs_now()
    x1 = P.op("vector", lambda e: e.tensor_copy(env.row(1, 0, 1), env.AF[0:1, 8260:8261]),
              waits=xw, inc=env.eV)
    x2 = P.op("tensor", lambda e: e.matmul(env.PS[5][0:128, 0:1], env.ones1(128),
              env.row(1, 0, 1), start=True, stop=True),
              waits=[(env.eV, x1)] + env.ps_wait(5), inc=env.eT)
    x3 = P.op("vector", lambda e: e.tensor_copy(env.AF[:, 8261:8262], env.PS[5][0:128, 0:1]),
              waits=[(env.eT, x2)], inc=env.eV)
    env.ps_done(5, env.eV, x3)
    xsa_col = env.AF[:, 8261:8262]

    for b in range(B):
        lnv = emit_ln_fm_batch(env, src, b, 0, 37)
        wq = load_W(env, wp + "WqT")
        produce_fm(env, 8, wq, lnv, scale=0.125)
        wk = load_W(env, wp + "WkT")
        produce_fm(env, 16, wk, None)
        wv = load_W(env, wp + "WvT")
        qkv_done = produce_tm(env, 24, wv, None)
        env._sm_guard = {}
        for h in range(16):
            for jt in range(NTB):
                emit_sm_attn(env, b, h, jt, qkv_done)
        o_all = env.eV.cnt
        cg = None
        for jt in range(NTB):
            otile = env.slab(jt // 2, (jt % 2) * 1024, 1024)
            vtile = env.slab(24 + jt // 2, (jt % 2) * 1024, 1024)
            prod = env.slab(38, 0, 1024)
            vsq = env.slab(38, 1024, 1024)
            dot16 = env.AF[:, 8480:8496]
            vns16 = env.AF[:, 8496:8512]
            alph = env.AF[:, 8512:8528]
            t0 = P.op("vector", lambda e, _o=prod, _a=otile, _b=vtile:
                      e.tensor_tensor(_o, _a, _b, ALU.mult),
                      waits=[(env.eV, o_all)] + ([cg] if cg else []), inc=env.eV)
            t1 = P.op("vector", lambda e, _o=dot16, _i=ap_mk(env.AB, 38 * 2048,
                      [[64, 16], [1, 64]]): e.reduce_sum(_o, _i, axis=AX.X),
                      waits=[(env.eV, t0)], inc=env.eV)
            s0 = P.op("scalar", lambda e, _o=vsq, _i=vtile: e.activation(
                _o, _i, AF.Square), waits=[(env.eV, t1)], inc=env.eS)
            t2 = P.op("vector", lambda e, _o=vns16, _i=ap_mk(env.AB, 38 * 2048 + 1024,
                      [[64, 16], [1, 64]]): e.reduce_sum(_o, _i, axis=AX.X),
                      waits=[(env.eS, s0)], inc=env.eV)
            t3 = P.op("vector", lambda e, _o=vns16: e.tensor_scalar(
                _o, _o, 1e-8, None, ALU.add), waits=[(env.eV, t2)], inc=env.eV)
            t4 = P.op("vector", lambda e, _o=alph, _i=vns16: e.reciprocal(_o, _i),
                      waits=[(env.eV, t3)], inc=env.eV)
            t5 = P.op("vector", lambda e, _o=alph, _i=dot16: e.tensor_tensor(
                _o, _o, _i, ALU.mult), waits=[(env.eV, t4)], inc=env.eV)
            t6 = P.op("vector", lambda e, _o=alph, _x=xsa_col: e.tensor_scalar(
                _o, _o, _x, -1.0, ALU.mult, ALU.mult), waits=[(env.eV, t5)], inc=env.eV)
            t7 = P.op("vector", lambda e, _o=prod, _v=vtile,
                      _a=ap_mk(env.AF, 8512, [[1, 16], [0, 64]]):
                      e.tensor_tensor(_o, _v, _a, ALU.mult),
                      waits=[(env.eV, t6)], inc=env.eV)
            cg = P.op("vector", lambda e, _o=otile, _p=prod: e.tensor_tensor(
                _o, _o, _p, ALU.add), waits=[(env.eV, t7)], inc=env.eV)
            cg = (env.eV, cg)
        corr_done = env.eV.cnt
        ofm = None
        for jt in range(NTB):
            dstap = ap_mk(env.AB, 16 * 2048 + jt * 128, [[2048, 8], [1, 128]])
            ofm = emit_transpose_8(env, env.slab(jt // 2, (jt % 2) * 1024, 1024),
                                   dstap, [(env.eV, corr_done)], pt_idx=jt % 2)
        wo = load_W(env, wp + "WoT")
        for jj in range(NTB):
            j = b * NTB + jj
            tgts = []
            for nch in range(2):
                bank = (jj % 2) * 2 + nch
                ps = env.PS[bank]
                for kc in range(8):
                    w = []
                    if kc == 0:
                        w = [(env.eV, ofm)] + wo + env.ps_wait(bank)
                    lhsT = env.slab(16 + kc, jj * 128, 128)
                    P.op("tensor", lambda e, _o=ps[:], _l=lhsT,
                         _r=wslice(env, kc, nch * 512, 512), _s=(kc == 0), _p=(kc == 7):
                         e.matmul(_o, _l, _r, start=_s, stop=_p),
                         waits=w, inc=env.eT if kc == 7 else None)
                tgts.append(env.eT.cnt)
            st, tv = emit_residual_store(
                env, src, dst, j,
                [env.PS[(jj % 2) * 2][:], env.PS[(jj % 2) * 2 + 1][:]],
                [[(env.eT, tgts[0])], [(env.eT, tgts[1])]])
            env.ps_done((jj % 2) * 2, env.eV, tv)
            env.ps_done((jj % 2) * 2 + 1, env.eV, tv)


def emit_l2b(env, src, dst, wp):
    P = env.P
    bqc = load_bias_cols(env, wp + "bqcol", 8256, 8)
    bkc = load_bias_cols(env, wp + "bkcol", 8264, 8)
    bvr = load_row(env, wp + "bvrow", 0)
    for b in range(B):
        lnv = emit_ln_fm_batch(env, src, b, 0, 37)
        wq = load_W(env, wp + "WqN")
        produce_fm(env, 8, wq + bqc, lnv, scale=0.125,
                   bias_fn=lambda oc: env.AF[:, 8256 + oc:8257 + oc])
        wk = load_W(env, wp + "WkN")
        produce_fm(env, 16, wk + bkc, None,
                   bias_fn=lambda oc: env.AF[:, 8264 + oc:8265 + oc])
        wv = load_W(env, wp + "WvN")
        for h in range(16):
            phi = h % 4
            for pt in range(4):
                bank = (h * 4 + pt) % 4
                ps = env.PS[bank]
                for kc in range(8):
                    w = []
                    if kc == 0:
                        w = wv + bvr + env.ps_wait(bank)
                    lhsT = ap_mk(env.AB, kc * 2048 + 512 * pt + phi, [[4, 128]])
                    P.op("tensor", lambda e, _o=ps[:, 0:64], _l=lhsT,
                         _r=wslice(env, kc, h * 64, 64), _s=(kc == 0):
                         e.matmul(_o, _l, _r, start=_s, stop=False),
                         waits=w)
                P.op("tensor", lambda e, _o=ps[:, 0:64], _l=env.ones1(128),
                     _r=env.row(0, h * 64, 64):
                     e.matmul(_o, _l, _r, start=False, stop=True), inc=env.eT)
                tmm = env.eT.cnt
                ev = P.op("vector", lambda e, _o=env.slab(24 + pt // 2,
                          (pt % 2) * 1024 + h * 64, 64), _i=ps[:, 0:64]:
                          e.tensor_copy(_o, _i), waits=[(env.eT, tmm)], inc=env.eV)
                env.ps_done(bank, env.eV, ev)
        svv = env.eV.cnt
        qk_done = svv
        zt = env.eT.cnt
        for s in range(8):
            P.op("vector", lambda e, _o=env.slab(26 + s, 0, 2048),
                 _i=ap_mk(env.consts, 2, [[0, 2048]]): e.tensor_copy(_o, _i),
                 waits=[(env.eT, zt)] if s == 0 else [], inc=env.eV)
        zdone = env.eV.cnt
        guard = {}
        for h in range(16):
            phi = h % 4
            for qt in range(4):
                par = (h * 4 + qt) % 2
                negm = env.AF[:, 8448 + par * 8:8449 + par * 8]
                den = env.AF[:, 8450 + par * 8:8451 + par * 8]
                rden = env.AF[:, 8452 + par * 8:8453 + par * 8]
                bank = (h * 4 + qt) % 4
                lhsT = ap_mk(env.AB, (8 + h // 2) * 2048 + 512 * qt + phi,
                             [[4, 128]], part0=(h % 2) * 64, nparts=64)
                rhs = ap_mk(env.AB, (16 + h // 2) * 2048 + phi, [[4, 512]],
                            part0=(h % 2) * 64, nparts=64)
                w = env.ps_wait(bank)
                if h == 0 and qt == 0:
                    w = w + [(env.eV, qk_done)]
                P.op("tensor", lambda e, _o=env.PS[bank][:], _l=lhsT, _r=rhs:
                     e.matmul(_o, _l, _r, start=True, stop=True),
                     waits=w, inc=env.eT)
                smm = env.eT.cnt
                sg = guard.get(par)
                mx = P.op("vector", lambda e, _o=negm, _i=env.PS[bank][:]:
                          e.reduce_max(_o, _i, axis=AX.X, negate=True),
                          waits=[(env.eT, smm)] + ([sg] if sg else []), inc=env.eV)
                px = P.op("scalar", lambda e, _b=negm, _d=den, _i=env.PS[bank][:]:
                          e.activation(env.slab(36, 0, 512), _i, AF.Exp, bias=_b,
                          accum_out=_d), waits=[(env.eV, mx)], inc=env.eS)
                guard[par] = (env.eS, px)
                env.ps_done(bank, env.eS, px)
                rd = P.op("vector", lambda e, _o=rden, _i=den: e.reciprocal(_o, _i),
                          waits=[(env.eS, px)], inc=env.eV)
                for c in range(4):
                    w2 = ([(env.eS, px)] + env.pt_wait(0)) if c == 0 else []
                    P.op("tensor", lambda e, _o=env.PT[0][:, c * 128:(c + 1) * 128],
                         _i=env.slab(36, c * 128, 128):
                         e.transpose(_o, _i, env.ident[:]), waits=w2,
                         inc=env.eT if c == 3 else None)
                tT = env.eT.cnt
                tev = P.op("vector", lambda e, _o=env.slab(37, 0, 512),
                           _i=env.PT[0][:, 0:512]: e.tensor_copy(_o, _i),
                           waits=[(env.eT, tT)], inc=env.eV)
                env.pt_done(0, env.eV, tev)
                for c in range(4):
                    w3 = []
                    if c == 0:
                        w3 = [(env.eV, tev)] + env.ps_wait(4)
                    P.op("tensor", lambda e, _o=env.PS[4][:, 0:64],
                         _l=env.slab(37, c * 128, 128),
                         _r=env.slab(24 + c // 2, (c % 2) * 1024 + h * 64, 64),
                         _s=(c == 0), _p=(c == 3):
                         e.matmul(_o, _l, _r, start=_s, stop=_p),
                         waits=w3, inc=env.eT if c == 3 else None)
                pvm = env.eT.cnt
                cv = P.op("vector", lambda e, _o=env.slab(36, 512, 64),
                          _i=env.PS[4][:, 0:64], _r=rden:
                          e.tensor_scalar(_o, _i, _r, None, ALU.mult),
                          waits=[(env.eT, pvm), (env.eV, rd)], inc=env.eV)
                env.ps_done(4, env.eV, cv)
                tr = P.op("tensor", lambda e, _o=env.PT[1][0:64, 0:128],
                          _i=env.slab(36, 512, 64):
                          e.transpose(_o, _i, env.ident[:]),
                          waits=[(env.eV, cv)] + env.pt_wait(1), inc=env.eT)
                sc = P.op("vector", lambda e, _o=ap_mk(env.AB,
                          (26 + h // 2) * 2048 + 512 * qt + phi, [[4, 128]],
                          part0=(h % 2) * 64, nparts=64),
                          _i=env.PT[1][0:64, 0:128]: e.tensor_copy(_o, _i),
                          waits=[(env.eT, tr)], inc=env.eV)
                env.pt_done(1, env.eV, sc)
        for jj in range(NTB):
            j = b * NTB + jj
            pt = env.PT[jj % 2]
            for s in range(8):
                w4 = env.pt_wait(jj % 2) if s == 0 else []
                P.op("tensor", lambda e, _o=pt[:, s * 128:(s + 1) * 128],
                     _i=env.slab(26 + s, jj * 128, 128):
                     e.transpose(_o, _i, env.ident[:]), waits=w4,
                     inc=env.eT if s == 7 else None)
            tT = env.eT.cnt
            st, tv = emit_residual_store(
                env, src, dst, j, [pt[:, 0:512], pt[:, 512:1024]],
                [[(env.eT, tT)], []])
            env.pt_done(jj % 2, env.eV, tv)


def prep_l2b(inputs, core, dram):
    Wq = np.asarray(inputs["L2_attn_b_Wq"], np.float32)
    bq = np.asarray(inputs["L2_attn_b_bq"], np.float32)
    Wkv = np.asarray(inputs["L2_attn_b_Wkv"], np.float32)
    bkv = np.asarray(inputs["L2_attn_b_bkv"], np.float32)
    dram["l2b_WqN"] = np.ascontiguousarray(Wq)
    dram["l2b_WkN"] = np.ascontiguousarray(Wkv[:, 0:1024])
    dram["l2b_WvN"] = np.ascontiguousarray(Wkv[:, 1024:2048])
    dram["l2b_bqcol"] = np.ascontiguousarray(bq.reshape(8, 128).T)
    dram["l2b_bkcol"] = np.ascontiguousarray(bkv[0:1024].reshape(8, 128).T)
    dram["l2b_bvrow"] = np.ascontiguousarray(bkv[1024:2048].reshape(1, 1024))


def prep_l2a(inputs, core, dram):
    for nm in ("Wq", "Wk", "Wv", "Wo"):
        dram["l2a_" + nm + "T"] = np.ascontiguousarray(
            np.asarray(inputs["L2_attn_a_" + nm], np.float32).T)
    dram["l2a_xsa"] = np.asarray(inputs["L2_attn_a_xsa_scale"], np.float32).reshape(1, 1)




def emit_l1f(env, src, dst, wp):
    P = env.P
    for b in range(B):
        lnv = emit_ln_fm_batch(env, src, b, 0, 37)
        prev_sel_reader = None
        for pi, p in enumerate(("q_", "k_", "v_")):
            pp = wp + p
            P.wait("gpsimd", [(env.eT, env.eT.cnt)])
            nwt = load_weight_bf16(env, pp + "nwT",
                                   [(38, 512 + 16 * k) for k in range(8)])
            P.wait("gpsimd", env.qg_now())
            nb = env.dma("gpsimd", lambda e, _pp=pp: e.dma_start(
                out=ap_mk(env.AB, 38 * 2048 + 704, [[1, 15]], nparts=1),
                in_=env.dram[_pp + "nbrow"]))
            b2w = load_weight_bf16(env, pp + "b2nat", [(38, 768)])
            b1r = load_row(env, pp + "b1row", 1)
            w1 = load_W(env, pp + "W1all")
            ndone = None
            for jj in range(NTB):
                bank = jj % 4
                ps = env.PS[bank]
                for kc in range(8):
                    w = []
                    if kc == 0:
                        w = nwt + [nb] + env.ps_wait(bank)
                        if jj == 0 and pi == 0:
                            w = w + [(env.eV, lnv)]
                    P.op("tensor", lambda e, _o=ps[:, 0:15],
                         _l=env.slab(kc, jj * 128, 128),
                         _r=ap_mk(env.AB, 38 * 2048 + 512 + 16 * kc, [[1, 15]]),
                         _s=(kc == 0): e.matmul(_o, _l, _r, start=_s, stop=False),
                         waits=w)
                P.op("tensor", lambda e, _o=ps[:, 0:15], _l=env.ones1(128),
                     _r=ap_mk(env.AB, 38 * 2048 + 704, [[1, 15]], nparts=1):
                     e.matmul(_o, _l, _r, start=False, stop=True), inc=env.eT)
                tmm = env.eT.cnt
                gg = P.op("vector", lambda e, _i=ps[:, 0:15]: e.tensor_scalar(
                    env.AF[:, 8608:8623], _i, 0.0, None, ALU.is_ge),
                    waits=[(env.eT, tmm)], inc=env.eV)
                env.ps_done(bank, env.eV, gg)
                hh_ = P.op("vector", lambda e: e.tensor_scalar(
                    env.AF[:, 8624:8639], env.AF[:, 8608:8623], -1.0, 1.0,
                    ALU.mult, ALU.add), waits=[(env.eV, gg)], inc=env.eV)

                def gh(node, bit):
                    base = 8608 if bit else 8624
                    return env.AF[:, base + node:base + node + 1]
                lv = hh_
                for c in range(4):
                    lv = P.op("vector", lambda e, _o=env.AF[:, 8640 + c:8641 + c],
                              _a=gh(0, c >> 1), _b=gh(1 + (c >> 1), c & 1):
                              e.tensor_tensor(_o, _a, _b, ALU.mult),
                              waits=[(env.eV, lv)], inc=env.eV)
                for c in range(8):
                    lv = P.op("vector", lambda e, _o=env.AF[:, 8648 + c:8649 + c],
                              _a=env.AF[:, 8640 + (c >> 1):8641 + (c >> 1)],
                              _b=gh(3 + (c >> 1), c & 1):
                              e.tensor_tensor(_o, _a, _b, ALU.mult),
                              waits=[(env.eV, lv)], inc=env.eV)
                for c in range(16):
                    lv = P.op("vector", lambda e, _o=env.AF[:, 8656 + c:8657 + c],
                              _a=env.AF[:, 8648 + (c >> 1):8649 + (c >> 1)],
                              _b=gh(7 + (c >> 1), c & 1):
                              e.tensor_tensor(_o, _a, _b, ALU.mult),
                              waits=[(env.eV, lv)], inc=env.eV)
                sw = [(env.eV, lv)]
                if prev_sel_reader is not None:
                    sw = sw + [prev_sel_reader]
                sb = P.op("vector", lambda e, _o=ap_mk(env.AB,
                          38 * 2048 + 1792 + jj * 16, [[1, 16]]),
                          _i=env.AF[:, 8656:8672]: e.tensor_copy(_o, _i),
                          waits=sw, inc=env.eV)
                tr = P.op("tensor", lambda e, _o=env.PT[jj % 2][0:16, 0:128],
                          _i=ap_mk(env.AB, 38 * 2048 + 1792 + jj * 16, [[1, 16]]):
                          e.transpose(_o, _i, env.ident[:]),
                          waits=[(env.eV, sb)] + env.pt_wait(jj % 2), inc=env.eT)
                sc = P.op("vector", lambda e, _o=env.selT[0:16, jj * 128:(jj + 1) * 128],
                          _i=env.PT[jj % 2][0:16, 0:128]: e.tensor_copy(_o, _i),
                          waits=[(env.eT, tr)], inc=env.eV)
                env.pt_done(jj % 2, env.eV, sc)
                rl = None
                for nch in range(2):
                    bank2 = 2 + (jj % 2)
                    ps2 = env.PS[bank2]
                    for kc in range(8):
                        w = []
                        if kc == 0:
                            w = w1 + b1r + env.ps_wait(bank2)
                        P.op("tensor", lambda e, _o=ps2[:],
                             _l=env.slab(kc, jj * 128, 128),
                             _r=wslice(env, kc, nch * 512, 512), _s=(kc == 0):
                             e.matmul(_o, _l, _r, start=_s, stop=False), waits=w)
                    P.op("tensor", lambda e, _o=ps2[:], _l=env.ones1(128),
                         _r=env.row(1, nch * 512, 512):
                         e.matmul(_o, _l, _r, start=False, stop=True), inc=env.eT)
                    tm2 = env.eT.cnt
                    rl = P.op("scalar", lambda e, _o=env.slab(36, (jj % 2) * 1024 +
                              nch * 512, 512), _i=ps2[:]:
                              e.activation(_o, _i, AF.Relu),
                              waits=[(env.eT, tm2)], inc=env.eS)
                    env.ps_done(bank2, env.eS, rl)
                mk = P.op("vector", lambda e, _o=env.slab(36, (jj % 2) * 1024, 1024),
                          _a=env.slab(36, (jj % 2) * 1024, 1024),
                          _b=ap_mk(env.AB, 38 * 2048 + 1792 + jj * 16,
                                   [[1, 16], [0, 64]]):
                          e.tensor_tensor(_o, _a, _b, ALU.mult),
                          waits=[(env.eS, rl), (env.eV, sb)], inc=env.eV)
                dstap = ap_mk(env.AB, 8 * 2048 + jj * 128, [[2048, 8], [1, 128]])
                ndone = emit_transpose_8(env, env.slab(36, (jj % 2) * 1024, 1024),
                                         dstap, [(env.eV, mk)], pt_idx=jj % 2)
            P.wait("gpsimd", [(env.eT, env.eT.cnt)])
            w2 = load_W(env, pp + "W2all")
            if pi < 2:
                out0 = 16 if pi == 0 else 24
                for oc in range(8):
                    for nch in range(4):
                        bank = (oc * 4 + nch) % 4
                        ps = env.PS[bank]
                        for kc in range(8):
                            w = []
                            if kc == 0:
                                w = w2 + b2w + [(env.eV, ndone)] + env.ps_wait(bank)
                            P.op("tensor", lambda e, _o=ps[:],
                                 _l=wslice(env, kc, oc * 128, 128),
                                 _r=env.slab(8 + kc, nch * 512, 512), _s=(kc == 0):
                                 e.matmul(_o, _l, _r, start=_s, stop=False),
                                 waits=w)
                        P.op("tensor", lambda e, _o=ps[:],
                             _l=ap_mk(env.AB, 38 * 2048 + 768 + oc * 128, [[1, 128]],
                                      nparts=16),
                             _r=env.selT[0:16, nch * 512:(nch + 1) * 512]:
                             e.matmul(_o, _l, _r, start=False, stop=True),
                             inc=env.eT)
                        tm3 = env.eT.cnt
                        if pi == 0:
                            ev = P.op("vector", lambda e, _o=env.slab(out0 + oc,
                                      nch * 512, 512), _i=ps[:]: e.tensor_scalar(
                                      _o, _i, 0.03125, None, ALU.mult),
                                      waits=[(env.eT, tm3)], inc=env.eV)
                        else:
                            ev = P.op("vector", lambda e, _o=env.slab(out0 + oc,
                                      nch * 512, 512), _i=ps[:]: e.tensor_copy(_o, _i),
                                      waits=[(env.eT, tm3)], inc=env.eV)
                        env.ps_done(bank, env.eV, ev)
            else:
                for jj in range(NTB):
                    for nch in range(2):
                        bank = (jj * 2 + nch) % 4
                        ps = env.PS[bank]
                        for kc in range(8):
                            w = []
                            if kc == 0:
                                w = w2 + b2w + [(env.eV, ndone)] + env.ps_wait(bank)
                            P.op("tensor", lambda e, _o=ps[:],
                                 _l=env.slab(8 + kc, jj * 128, 128),
                                 _r=wslice(env, kc, nch * 512, 512), _s=(kc == 0):
                                 e.matmul(_o, _l, _r, start=_s, stop=False),
                                 waits=w)
                        P.op("tensor", lambda e, _o=ps[:],
                             _l=env.selT[0:16, jj * 128:(jj + 1) * 128],
                             _r=ap_mk(env.AB, 38 * 2048 + 768 + nch * 512, [[1, 512]],
                                      nparts=16):
                             e.matmul(_o, _l, _r, start=False, stop=True),
                             inc=env.eT)
                        tm3 = env.eT.cnt
                        ev = P.op("vector", lambda e, _o=env.slab(jj // 2,
                                  (jj % 2) * 1024 + nch * 512, 512), _i=ps[:]:
                                  e.tensor_copy(_o, _i),
                                  waits=[(env.eT, tm3)], inc=env.eV)
                        env.ps_done(bank, env.eV, ev)
            prev_sel_reader = (env.eT, env.eT.cnt)
        prod_done = env.eV.cnt
        guard = {}
        for qt in range(NTB):
            par = qt % 2
            negm = env.AF[:, 8448 + par * 8:8449 + par * 8]
            den = env.AF[:, 8450 + par * 8:8451 + par * 8]
            rden = env.AF[:, 8452 + par * 8:8453 + par * 8]
            for bank in range(4):
                for kc in range(8):
                    w = []
                    if kc == 0:
                        w = env.ps_wait(bank)
                        if qt == 0 and bank == 0:
                            w = w + [(env.eV, prod_done)]
                    P.op("tensor", lambda e, _o=env.PS[bank][:],
                         _l=env.slab(16 + kc, qt * 128, 128),
                         _r=env.slab(24 + kc, bank * 512, 512), _s=(kc == 0),
                         _p=(kc == 7): e.matmul(_o, _l, _r, start=_s, stop=_p),
                         waits=w, inc=env.eT if kc == 7 else None)
            smm = env.eT.cnt
            sg = guard.get(par)
            mx = P.op("vector", lambda e, _o=negm: e.reduce_max(
                _o, env.PSB[:, 0:2048], axis=AX.X, negate=True),
                waits=[(env.eT, smm)] + ([sg] if sg else []), inc=env.eV)
            px = P.op("scalar", lambda e, _b=negm, _d=den: e.activation(
                env.slab(36, 0, 2048), env.PSB[:, 0:2048], AF.Exp, bias=_b,
                accum_out=_d), waits=[(env.eV, mx)], inc=env.eS)
            guard[par] = (env.eS, px)
            for bank in range(4):
                env.ps_done(bank, env.eS, px)
            rd = P.op("vector", lambda e, _o=rden, _i=den: e.reciprocal(_o, _i),
                      waits=[(env.eS, px)], inc=env.eV)
            for half in range(2):
                pt = env.PT[half]
                for k8 in range(8):
                    kc = half * 8 + k8
                    w = ([(env.eS, px)] + env.pt_wait(half)) if k8 == 0 else []
                    P.op("tensor", lambda e, _o=pt[:, k8 * 128:(k8 + 1) * 128],
                         _i=env.slab(36, kc * 128, 128):
                         e.transpose(_o, _i, env.ident[:]), waits=w,
                         inc=env.eT if k8 == 7 else None)
                tT = env.eT.cnt
                tev = P.op("vector", lambda e, _o=env.slab(37, half * 1024, 1024),
                           _i=pt[:, 0:1024]: e.tensor_copy(_o, _i),
                           waits=[(env.eT, tT)], inc=env.eV)
                env.pt_done(half, env.eV, tev)
            ptd = env.eV.cnt
            pvt = []
            for nch in range(2):
                psb = env.PS[4 + nch]
                for kc in range(16):
                    w = []
                    if kc == 0:
                        w = [(env.eV, ptd)] + env.ps_wait(4 + nch)
                    P.op("tensor", lambda e, _o=psb[:],
                         _l=env.slab(37, kc * 128, 128),
                         _r=env.slab(kc // 2, (kc % 2) * 1024 + nch * 512, 512),
                         _s=(kc == 0), _p=(kc == 15):
                         e.matmul(_o, _l, _r, start=_s, stop=_p),
                         waits=w, inc=env.eT if kc == 15 else None)
                pvt.append(env.eT.cnt)
            j = b * NTB + qt
            xt = env.xt(j)
            ld = env.dma("sync", lambda e, _x=xt, _s=src, _j=j: e.dma_start(
                out=_x, in_=_s[_j * 128:(_j + 1) * 128, :]),
                waits=env.xt_wait(j), slot=("x", j))
            tv = None
            for half in range(2):
                ww = [ld, (env.eT, pvt[half]), (env.eV, rd)]
                if half == 0:
                    ww += env.ot_wait(j)
                tv = P.op("vector", lambda e, _o=env.ot(j)[:, half * 512:(half + 1) * 512],
                          _p=env.PS[4 + half][:], _x=xt[:, half * 512:(half + 1) * 512],
                          _r=rden: e.scalar_tensor_tensor(_o, _p, _r, _x,
                          ALU.mult, ALU.add), waits=ww, inc=env.eV)
                env.ps_done(4 + half, env.eV, tv)
            env.xt_done(j, env.eV, tv)
            st = env.dma("sync", lambda e, _o=env.ot(j), _d=dst, _j=j: e.dma_start(
                out=_d[_j * 128:(_j + 1) * 128, :], in_=_o), waits=[(env.eV, tv)],
                slot=("o", j))
            env.ot_done(j, st[0], st[1])


def prep_l1f(inputs, core, dram):
    for p in ("q_", "k_", "v_"):
        g = lambda n, _p=p: np.asarray(inputs["L1_ffn_" + _p + n], np.float32)
        dram["l1f_" + p + "nwT"] = np.ascontiguousarray(g("node_w").T)
        dram["l1f_" + p + "nbrow"] = np.ascontiguousarray(g("node_b").reshape(1, 15))
        dram["l1f_" + p + "W1all"] = np.ascontiguousarray(
            g("w1").transpose(1, 0, 2).reshape(1024, 1024))
        dram["l1f_" + p + "b1row"] = np.ascontiguousarray(g("b1").reshape(1, 1024))
        dram["l1f_" + p + "W2all"] = np.ascontiguousarray(g("w2").reshape(1024, 1024))
        dram["l1f_" + p + "b2nat"] = np.ascontiguousarray(g("b2"))



def emit_l3a(env, src, dst, wp):
    P = env.P
    bqr = load_row(env, wp + "bqrow", 1)
    bkr = load_row(env, wp + "bkrow", 3)
    bor = load_row(env, wp + "borow", 2)
    tmt = load_weight_bf16(env, wp + "tmtables", [(38, 0)])
    trl = load_weight_bf16(env, wp + "tril128", [(38, 1024)])
    wcl = load_bias_cols(env, wp + "wcolall", 8704, 16)
    P.wait("gpsimd", env.qg_now())
    r1 = env.dma("gpsimd", lambda e: e.dma_start(
        out=ap_mk(env.AB, 35 * 2048, [[1, 2048]], nparts=1),
        in_=env.dram[wp + "vbrow"]))
    r2 = env.dma("gpsimd", lambda e: e.dma_start(
        out=ap_mk(env.AB, 34 * 2048, [[1, 2048]], nparts=1),
        in_=env.dram[wp + "gbrow"]))
    rws = [r1, r2]
    osc = env.nc.dram_tensor("l3a_osc", [2048, 2048], F32, kind="Internal").ap()
    for b in range(B):
        lnv = emit_ln_fm_batch(env, src, b, 0, 37)
        wq = load_W(env, wp + "WqP")
        produce_tm_rope_fm(env, 8, wq + bqr + tmt, lnv, None,
                           lambda nch: env.row(1, nch * 512, 512),
                           38 * 2048, 38 * 2048 + 512)
        wk = load_W(env, wp + "WkP")
        produce_tm_rope_fm(env, 16, wk + bkr, None, 0.125,
                           lambda nch: env.row(3, nch * 512, 512),
                           38 * 2048, 38 * 2048 + 512)
        qkr_done = env.eV.cnt
        P.wait("sync", env.qg_now())
        for h in range(16):
            P.wait("sync", [(env.eV, env.eV.cnt)])
            scw = load_bias_cols(env, wp + "scol%d" % h, 8720, 16)
            rfw = load_bias_cols(env, wp + "rowfac%d" % h, 8736, 16)
            P.wait("gpsimd", [(env.eT, env.eT.cnt)])
            wvh = load_weight_bf16(env, wp + "Wvc%d" % h,
                                   [(27, 128 * k) for k in range(8)])
            wgh = load_weight_bf16(env, wp + "Wgc%d" % h,
                                   [(27, 1024 + 128 * k) for k in range(8)])
            for jj in range(NTB):
                for (which, wcol0, dsl) in ((0, 0, None), (1, 1024, 26)):
                    bank = (jj * 2 + which) % 4
                    ps = env.PS[bank]
                    for kc in range(8):
                        w = []
                        if kc == 0:
                            w = wvh + wgh + rws + scw + rfw + wcl + trl + \
                                env.ps_wait(bank)
                            if jj == 0 and which == 0:
                                w = w + [(env.eV, qkr_done)]
                        lhsT = env.slab(kc, jj * 128, 128)
                        rhs = ap_mk(env.AB, 27 * 2048 + wcol0 + 128 * kc, [[1, 128]])
                        P.op("tensor", lambda e, _o=ps[:, 0:128], _l=lhsT, _r=rhs,
                             _s=(kc == 0): e.matmul(_o, _l, _r, start=_s, stop=False),
                             waits=w)
                    brow = ap_mk(env.AB, (35 if which == 0 else 34) * 2048 + h * 128,
                                 [[1, 128]], nparts=1)
                    P.op("tensor", lambda e, _o=ps[:, 0:128], _l=env.ones1(128),
                         _r=brow: e.matmul(_o, _l, _r, start=False, stop=True),
                         inc=env.eT)
                    tmm = env.eT.cnt
                    if which == 0:
                        dstap = ap_mk(env.AB, (24 + jj // 8) * 2048 + (jj % 8) * 129,
                                      [[1, 128]])
                        ev = P.op("vector", lambda e, _o=dstap, _i=ps[:, 0:128],
                                  _w=env.AF[:, 8704 + h:8705 + h]:
                                  e.tensor_scalar(_o, _i, _w, None, ALU.mult),
                                  waits=[(env.eT, tmm)], inc=env.eV)
                    else:
                        ev = P.op("vector", lambda e, _o=env.slab(26, jj * 128, 128),
                                  _i=ps[:, 0:128]: e.tensor_copy(_o, _i),
                                  waits=[(env.eT, tmm)], inc=env.eV)
                    env.ps_done(bank, env.eV, ev)
            wcg = env.eT.cnt
            wc = None
            for sslab in (24, 25):
                wc = P.op("vector", lambda e, _o=ap_mk(env.AB, sslab * 2048 + 128,
                          [[129, 8], [1, 1]]),
                          _i=ap_mk(env.AF, 8704 + h, [[0, 8], [0, 1]]):
                          e.tensor_copy(_o, _i), waits=[(env.eT, wcg)], inc=env.eV)
            vg_done = env.eV.cnt
            for mt in range(16):
                nch_ = mt + 1
                used = set()
                for nt in range(nch_):
                    bank = nt // 4
                    w = []
                    if bank not in used:
                        used.add(bank)
                        w = env.ps_wait(bank)
                        if mt == 0:
                            w = w + [(env.eV, vg_done)]
                    lhsT = ap_mk(env.AB, (8 + h // 2) * 2048 + mt * 128, [[1, 128]],
                                 part0=(h % 2) * 64, nparts=64)
                    rhs = ap_mk(env.AB, (16 + h // 2) * 2048 + nt * 128, [[1, 128]],
                                part0=(h % 2) * 64, nparts=64)
                    P.op("tensor", lambda e, _o=env.PSB[:, nt * 128:(nt + 1) * 128],
                         _l=lhsT, _r=rhs: e.matmul(_o, _l, _r, start=True, stop=True),
                         waits=w, inc=env.eT)
                smm = env.eT.cnt
                pv_ev = None
                for nt in range(nch_):
                    w = [(env.eT, smm)] if nt == 0 else []
                    if nt == mt:
                        pv_ev = P.op("vector", lambda e,
                                     _o=env.slab(36, nt * 128, 128),
                                     _a=env.PSB[:, nt * 128:(nt + 1) * 128],
                                     _b=env.slab(38, 1024, 128):
                                     e.tensor_tensor(_o, _a, _b, ALU.mult),
                                     waits=w, inc=env.eV)
                    else:
                        pv_ev = P.op("vector", lambda e,
                                     _o=env.slab(36, nt * 128, 128),
                                     _a=env.PSB[:, nt * 128:(nt + 1) * 128],
                                     _s=env.AF[:, 8720 + mt - nt:8721 + mt - nt]:
                                     e.tensor_scalar(_o, _a, _s, None, ALU.mult),
                                     waits=w, inc=env.eV)
                pall = env.eV.cnt
                for bank in sorted(used):
                    env.ps_done(bank, env.eV, pall)
                tev = None
                for half in range((nch_ + 7) // 8):
                    blk = min(8, nch_ - half * 8)
                    pt = env.PT[half]
                    for k8 in range(blk):
                        nt = half * 8 + k8
                        w = ([(env.eV, pall)] + env.pt_wait(half)) if k8 == 0 else []
                        P.op("tensor", lambda e, _o=pt[:, k8 * 128:(k8 + 1) * 128],
                             _i=env.slab(36, nt * 128, 128):
                             e.transpose(_o, _i, env.ident[:]), waits=w,
                             inc=env.eT if k8 == blk - 1 else None)
                    tT = env.eT.cnt
                    tev = P.op("vector", lambda e, _o=env.slab(37, half * 1024,
                               blk * 128), _i=pt[:, 0:blk * 128]:
                               e.tensor_copy(_o, _i), waits=[(env.eT, tT)],
                               inc=env.eV)
                    env.pt_done(half, env.eV, tev)
                ptd = env.eV.cnt
                for nt in range(nch_):
                    w = []
                    if nt == 0:
                        w = [(env.eV, ptd)] + env.ps_wait(4)
                    rhs = ap_mk(env.AB, (24 + nt // 8) * 2048 + (nt % 8) * 129,
                                [[1, 129]])
                    P.op("tensor", lambda e, _o=env.PS[4][:, 0:129],
                         _l=env.slab(37, nt * 128, 128), _r=rhs,
                         _s=(nt == 0), _p=(nt == nch_ - 1):
                         e.matmul(_o, _l, _r, start=_s, stop=_p),
                         waits=w, inc=env.eT if nt == nch_ - 1 else None)
                pvm = env.eT.cnt
                rfc = env.AF[:, 8736 + mt:8737 + mt]
                dn = P.op("vector", lambda e: e.tensor_copy(
                    env.AF[:, 8752:8753], env.PS[4][:, 128:129]),
                    waits=[(env.eT, pvm)], inc=env.eV)
                a1 = P.op("vector", lambda e, _r=rfc: e.tensor_scalar(
                    env.AF[:, 8753:8754], env.AF[:, 8752:8753], _r, None, ALU.mult),
                    waits=[(env.eV, dn)], inc=env.eV)
                ab = P.op("scalar", lambda e: e.activation(
                    env.AF[:, 8754:8755], env.AF[:, 8753:8754], AF.Abs),
                    waits=[(env.eV, a1)], inc=env.eS)
                c1 = P.op("vector", lambda e: e.tensor_scalar(
                    env.AF[:, 8754:8755], env.AF[:, 8754:8755], 1.0, None, ALU.max),
                    waits=[(env.eS, ab)], inc=env.eV)
                rc = P.op("vector", lambda e: e.reciprocal(
                    env.AF[:, 8755:8756], env.AF[:, 8754:8755]),
                    waits=[(env.eV, c1)], inc=env.eV)
                fs = P.op("vector", lambda e, _r=rfc: e.tensor_scalar(
                    env.AF[:, 8755:8756], env.AF[:, 8755:8756], _r, None, ALU.mult),
                    waits=[(env.eV, rc)], inc=env.eV)
                fx = P.op("scalar", lambda e: e.copy(
                    env.AF[:, 8756:8757], env.AF[:, 8755:8756]),
                    waits=[(env.eV, fs)], inc=env.eS)
                oe = P.op("vector", lambda e: e.tensor_scalar(
                    env.slab(28, 0, 128), env.PS[4][:, 0:128],
                    env.AF[:, 8756:8757], None, ALU.mult),
                    waits=[(env.eS, fx)], inc=env.eV)
                env.ps_done(4, env.eV, oe)
                bns = P.op("vector", lambda e: e.bn_stats(
                    env.stat(0, 6), env.slab(28, 0, 128)),
                    waits=[(env.eV, oe)], inc=env.eV)
                bna = P.op("vector", lambda e: e.bn_aggr(
                    env.stat(12, 2), env.stat(0, 6)),
                    waits=[(env.eV, bns)], inc=env.eV)
                s1 = P.op("scalar", lambda e: e.activation(
                    env.stat(14), env.stat(12, 2)[:, 1:2], AF.Ln, bias=EPS_LN),
                    waits=[(env.eV, bna)], inc=env.eS)
                s2 = P.op("scalar", lambda e: e.activation(
                    env.stat(15), env.stat(14), AF.Exp, scale=-0.5),
                    waits=[(env.eS, s1)], inc=env.eS)
                gw = [(env.eV, vg_done)] if mt == 0 else []
                gsl = P.op("scalar", lambda e, _i=env.slab(26, mt * 128, 128):
                           e.activation(env.slab(28, 512, 128), _i, AF.Silu),
                           waits=gw + [(env.eS, s2)], inc=env.eS)
                lnx = P.op("vector", lambda e: e.tensor_scalar(
                    env.slab(28, 256, 128), env.slab(28, 0, 128),
                    env.stat(12, 2)[:, 0:1], env.stat(15), ALU.subtract, ALU.mult),
                    waits=[(env.eS, s2)], inc=env.eV)
                gt = P.op("vector", lambda e: e.tensor_tensor(
                    env.slab(28, 256, 128), env.slab(28, 256, 128),
                    env.slab(28, 512, 128), ALU.mult),
                    waits=[(env.eS, gsl), (env.eV, lnx)], inc=env.eV)
                tr = P.op("tensor", lambda e: e.transpose(
                    env.PT[1][:, 0:128], env.slab(28, 256, 128), env.ident[:]),
                    waits=[(env.eV, gt)] + env.pt_wait(1), inc=env.eT)
                sg = env._af_guard.get("l3a")
                stg = P.op("vector", lambda e: e.tensor_copy(
                    env.AF[:, 8528:8656], env.PT[1][:, 0:128]),
                    waits=[(env.eT, tr)] + ([sg] if sg else []), inc=env.eV)
                env.pt_done(1, env.eV, stg)
                std = env.dma("sync", lambda e, _o=osc[h * 128:(h + 1) * 128,
                              mt * 128:(mt + 1) * 128], _i=env.AF[:, 8528:8656]:
                              e.dma_start(out=_o, in_=_i),
                              waits=[(env.eV, stg)], slot=("f", 0))
                env._af_guard["l3a"] = std
        P.wait("gpsimd", [(env.eT, env.eT.cnt), (env.eV, env.eV.cnt)])
        P.wait("gpsimd", [(s, s.cnt) for s in env.qF if s.cnt > 0])
        for ch in range(16):
            dstap = ap_mk(env.AB, (8 + ch) * 2048, [[1, 2048]])
            env.dma("gpsimd", lambda e, _d=dstap, _s=osc[ch * 128:(ch + 1) * 128, :]:
                    e.dma_start(out=_d, in_=_s))
        orel = env.qg_now()
        wo = load_weight_bf16(env, wp + "OwN",
                              [(24 + r // 2, (r % 2) * 1024) for r in range(16)])
        for jj in range(NTB):
            j = b * NTB + jj
            tgts = []
            for nch in range(2):
                bank = (jj % 2) * 2 + nch
                ps = env.PS[bank]
                for kc in range(16):
                    w = []
                    if kc == 0:
                        w = list(orel) + wo + bor + env.ps_wait(bank)
                    lhsT = env.slab(8 + kc, jj * 128, 128)
                    rhs = ap_mk(env.AB, (24 + kc // 2) * 2048 + (kc % 2) * 1024 +
                                nch * 512, [[1, 512]])
                    P.op("tensor", lambda e, _o=ps[:], _l=lhsT, _r=rhs,
                         _s=(kc == 0): e.matmul(_o, _l, _r, start=_s, stop=False),
                         waits=w)
                P.op("tensor", lambda e, _o=ps[:], _l=env.ones1(128),
                     _r=env.row(2, nch * 512, 512):
                     e.matmul(_o, _l, _r, start=False, stop=True), inc=env.eT)
                tgts.append(env.eT.cnt)
            st, tv = emit_residual_store(
                env, src, dst, j,
                [env.PS[(jj % 2) * 2][:], env.PS[(jj % 2) * 2 + 1][:]],
                [[(env.eT, tgts[0])], [(env.eT, tgts[1])]])
            env.ps_done((jj % 2) * 2, env.eV, tv)
            env.ps_done((jj % 2) * 2 + 1, env.eV, tv)


def prep_l3a(inputs, core, dram):
    g = lambda n: np.asarray(inputs["L3_attn_a_" + n], np.float32)
    idx64 = np.concatenate([np.arange(0, 64, 2), np.arange(1, 64, 2)])
    perm = np.concatenate([64 * hh + idx64 for hh in range(16)])
    dram["l3a_WqP"] = np.ascontiguousarray(g("q_w")[:, perm])
    dram["l3a_WkP"] = np.ascontiguousarray(g("k_w")[:, perm])
    dram["l3a_bqrow"] = np.ascontiguousarray(g("q_b")[perm].reshape(1, 1024))
    dram["l3a_bkrow"] = np.ascontiguousarray(g("k_b")[perm].reshape(1, 1024))
    dram["l3a_borow"] = np.ascontiguousarray(g("o_b").reshape(1, 1024))
    dram["l3a_vbrow"] = np.ascontiguousarray(g("v_b").reshape(1, 2048))
    dram["l3a_gbrow"] = np.ascontiguousarray(g("g_b").reshape(1, 2048))
    vw = g("v_w")
    gw = g("g_w")
    for hh in range(16):
        dram["l3a_Wvc%d" % hh] = np.ascontiguousarray(vw[:, hh * 128:(hh + 1) * 128])
        dram["l3a_Wgc%d" % hh] = np.ascontiguousarray(gw[:, hh * 128:(hh + 1) * 128])
    dram["l3a_OwN"] = np.ascontiguousarray(g("o_w"))
    pos = np.arange(2048, dtype=np.float64)[:, None]
    half = 10000.0 ** (-np.linspace(0.0, 1.0, 32, dtype=np.float64))[None, :]
    fr = pos * half
    c_vec = np.cos(fr)
    s_vec = np.sin(fr)
    ctm = np.zeros((128, 512), np.float64)
    stm = np.zeros((128, 512), np.float64)
    for jt in range(16):
        ctm[:, 32 * jt:32 * jt + 32] = c_vec[128 * jt:128 * (jt + 1), :]
        stm[:, 32 * jt:32 * jt + 32] = s_vec[128 * jt:128 * (jt + 1), :]
    dram["l3a_tmtables"] = np.concatenate([ctm, stm], axis=1).astype(BF)
    ii = np.arange(128)
    dram["l3a_tril128"] = None
    decay = np.log(1.0 - 2.0 ** (-5.0 - np.arange(16, dtype=np.float64)))
    tril = (ii[:, None] >= ii[None, :]).astype(np.float64)
    dram["l3a_tril128"] = tril.astype(BF)
    diff = np.arange(2048, dtype=np.float64)
    for hh in range(16):
        d = decay[hh]
        wcol = np.exp(-ii * d)
        if hh == 0:
            wall = np.zeros((128, 16), np.float32)
        wall[:, hh] = wcol.astype(np.float32)
        scol = np.exp(np.arange(16, dtype=np.float64) * 128.0 * d)
        dram["l3a_scol%d" % hh] = np.ascontiguousarray(
            np.broadcast_to(scol.astype(np.float32), (128, 16)).copy())
        m = np.arange(2048, dtype=np.float64)
        msum = np.zeros(2048, np.float64)
        ed = np.exp(d)
        msum = (1.0 - np.exp(d * (m + 1.0))) / (1.0 - ed)
        rowfac = np.exp((m % 128) * d) / np.sqrt(msum)
        dram["l3a_rowfac%d" % hh] = np.ascontiguousarray(
            rowfac.reshape(16, 128).T.astype(np.float32))
    dram["l3a_wcolall"] = wall



def prep_l2ffn(inputs, core, dram):
    g = core
    dram["l2f_W1T"] = np.ascontiguousarray(inputs["L2_ffn_W1"][g].T)
    dram["l2f_W2T"] = np.ascontiguousarray(inputs["L2_ffn_W2"][g].T)
    dram["l2f_b1col"] = np.ascontiguousarray(inputs["L2_ffn_b1"][g].reshape(32, 128).T)
    dram["l2f_b2row"] = np.ascontiguousarray(inputs["L2_ffn_b2"][g].reshape(1, 1024))


SUBLAYERS = [
    ("l0a", emit_l0a, prep_l0a_make("L0_attn_a_", "l0a_"), "l0a_"),
    ("l0f", emit_l0f, prep_l0f_make("L0_ffn_", "l0f_"), "l0f_"),
    ("l0b", emit_l0a, prep_l0a_make("L0_attn_b_", "l0b_"), "l0b_"),
    ("l0f2", emit_l0f, prep_l0f_make("L0_ffn_", "l0f_"), "l0f_"),
    ("l1a", emit_l1a, prep_l1a, "l1a_"),
    ("l1f", emit_l1f, prep_l1f, "l1f_"),
    ("l1b", emit_l1b, prep_l1b, "l1b_"),
    ("l1f2", emit_l1f, prep_l1f, "l1f_"),
    ("l2a", emit_l2a, prep_l2a, "l2a_"),
    ("l2f", emit_l2ffn, prep_l2ffn, "l2f_"),
    ("l2b", emit_l2b, prep_l2b, "l2b_"),
    ("l2f2", emit_l2ffn, prep_l2ffn, "l2f_"),
    ("l3a", emit_l3a, prep_l3a, "l3a_"),
    ("l3f", emit_l0f, prep_l0f_make("L3_ffn_", "l3f_"), "l3f_"),
    ("l3b", emit_l3b, prep_l3b, "l3b_"),
    ("l3f2", emit_l0f, prep_l0f_make("L3_ffn_", "l3f_"), "l3f_"),
]


def const_tables():
    return {
        "identity128": np.eye(128, dtype=BF),
        "ones512": np.ones((1, 512), dtype=BF),
        "constf32": np.tile(np.array([CONST_LIST], dtype=np.float32), (128, 1)),
        "onescol": np.ones((128, 8), dtype=BF),
    }


def build_full(seq, prepped, debug=False):
    reg = {s[0]: s for s in SUBLAYERS}
    nc = bass.Bass("TRN2", target_bir_lowering=False)
    es = ExitStack()
    dram = {}
    dram["x_in"] = nc.dram_tensor("x_in", [T, D], F32, kind="ExternalInput").ap()
    dram["y_out"] = nc.dram_tensor("y_out", [T, D], F32, kind="ExternalOutput").ap()
    dram["xs_a"] = nc.dram_tensor("xs_a", [T, D], F32, kind="Internal").ap()
    dram["xs_b"] = nc.dram_tensor("xs_b", [T, D], F32, kind="Internal").ap()
    for name, arr in prepped.items():
        dt = BF16 if arr.dtype == BF else F32
        dram[name] = nc.dram_tensor(name, list(arr.shape), dt, kind="ExternalInput").ap()
    P = Prog()
    env = Env(nc, P, es, dram)
    env.debug = debug
    bufs = ["x_in"] + (["xs_a", "xs_b"] * len(seq))
    for i, name in enumerate(seq):
        _, emit, _, wp = reg[name]
        src = dram["x_in"] if i == 0 else dram["xs_a" if i % 2 == 1 else "xs_b"]
        dst = dram["y_out"] if i == len(seq) - 1 else dram["xs_a" if i % 2 == 0 else "xs_b"]
        emit(env, src, dst, wp)
        emit_barrier(env)
    P.wait("sync", env.all_dma_waits())
    P.replay(nc, es)
    return nc, es, dram


def host_prep(inputs, seq):
    reg = {s[0]: s for s in SUBLAYERS}
    x = np.asarray(inputs["L0_attn_a_x"], dtype=np.float32)
    consts = const_tables()
    maps = []
    for c in range(NCORES):
        m = {}
        m["x_in"] = np.ascontiguousarray(x[2 * c:2 * c + 2].reshape(T, D))
        m.update(consts)
        for name in seq:
            _, _, prep, _ = reg[name]
            prep(inputs, c, m)
        maps.append(m)
    return maps


def run_seq(inputs, seq, debug=False, want_raw=False):
    maps = host_prep(inputs, seq)
    nc, es, dram = build_full(seq, {k: v for k, v in maps[0].items() if k != "x_in"},
                              debug=debug)
    with es:
        res = run_bass_kernel_spmd(nc, maps, core_ids=list(range(NCORES)))
    out = np.concatenate([np.asarray(r["y_out"]).reshape(B, N, D) for r in res.results], axis=0)
    if want_raw:
        return out, res
    return out


def kernel(**inputs):
    seq = [s[0] for s in SUBLAYERS]
    return run_seq(inputs, seq)
